# version 08
# baseline (speedup 1.0000x reference)
import numpy as np
import concourse.bass as bass
import concourse.mybir as mybir
from concourse.bass_utils import run_bass_kernel_spmd

F32 = mybir.dt.float32
BF16 = mybir.dt.bfloat16
AF = mybir.ActivationFunctionType
ALU = mybir.AluOpType
AX = mybir.AxisListType

D = 1024
TOK = 2048
NT = 16
KC = 8
NE = 16
NU = 18
NG = NU // 2
SEQ = 4096
EPS = 1e-5
RG = [[0, 1], [2, 3], [4, 5], [6, 7]]

PARTS = "full"
FAKE_CC = False
NQLIM = None

BASE = 16512
O_SCR = BASE + 2304
O_WRB = O_SCR + 13312
O_UT = O_SCR + 13568
O_X = O_UT + 32768
O_W = O_X + 65536


class Sem:
    def __init__(self, name):
        self.name = name
        self.v = 0
        self.h = None

    def now(self):
        return (self, self.v)


class Builder:
    def __init__(self):
        self.ops = []
        self.mile = {}

    def op(self, eng, emit, waits=(), incs=()):
        self.ops.append((eng, tuple(waits), emit, tuple(incs)))

    def resolve(self, w):
        if isinstance(w, str):
            return self.mile[w]
        return w


def build_program():
    nc = bass.Bass("TRN2", target_bir_lowering=False)

    xin = nc.dram_tensor("xin", [TOK, D], F32, kind="ExternalInput")
    wqkv = nc.dram_tensor("wqkv", [3 * D, D], F32, kind="ExternalInput")
    wrh = nc.dram_tensor("wr", [D, NE], F32, kind="ExternalInput")
    wgh = nc.dram_tensor("wgh", [NE, D, 512], F32, kind="ExternalInput")
    wuh = nc.dram_tensor("wuh", [NE, D, 512], F32, kind="ExternalInput")
    wdh = nc.dram_tensor("wdh", [NE, 512, D], F32, kind="ExternalInput")
    sgh = nc.dram_tensor("sgh", [D, D], F32, kind="ExternalInput")
    suh = nc.dram_tensor("suh", [D, D], F32, kind="ExternalInput")
    sdh = nc.dram_tensor("sdh", [D, D], F32, kind="ExternalInput")
    xout = nc.dram_tensor("xout", [TOK, D], F32, kind="ExternalOutput")

    xspill = nc.dram_tensor("xspill", [TOK, D], F32, kind="Internal")
    wcache = nc.dram_tensor("wcache", [NU * 128, 12288], BF16, kind="Internal")
    x2spill = nc.dram_tensor("x2spill", [TOK, D], F32, kind="Internal")
    agk_in = [nc.dram_tensor(f"agk_in{i}", [128, KC * TOK // 4], BF16, kind="Internal")
              for i in range(4)]
    agk_out = [nc.dram_tensor(f"agk_out{i}", [256, KC * TOK // 4], BF16, kind="Internal")
               for i in range(4)]
    agv_in = [nc.dram_tensor(f"agv_in{i}", [128, NT * D // 2], BF16, kind="Internal")
              for i in range(2)]
    agv_out = [nc.dram_tensor(f"agv_out{i}", [256, NT * D // 2], BF16, kind="Internal")
               for i in range(2)]

    at = nc.alloc_sbuf_tensor_at
    ident = at("ident", [128, 128], BF16, offset=BASE)
    lnstat = at("lnstat", [128, 128], F32, offset=BASE + 256)
    sstat = at("sstat", [128, 32], F32, offset=BASE + 768)
    probs = at("probs", [128, NE], F32, offset=BASE + 896)
    rtmp = at("rtmp", [128, NE], F32, offset=BASE + 960)
    cw = at("cw", [128, NT * NE], F32, offset=BASE + 1024)
    wrb = at("wrb", [128, KC * NE], BF16, offset=O_WRB)

    u_tile = at("u_tile", [128, 2 * D], BF16, offset=O_SCR)
    sil = at("sil", [128, 2 * 512], F32, offset=O_SCR + 4096)
    wstage = at("wstage", [128, 2 * D], BF16, offset=O_SCR + 4096)
    hbuf = at("hbuf", [128, 2 * 512], BF16, offset=O_SCR + 8192)
    hT = at("hT", [128, 3 * 512], BF16, offset=O_SCR + 10240)
    x1s = at("x1s", [128, 2 * D], F32, offset=O_SCR)
    x2st = at("x2st", [128, D], F32, offset=O_SCR + 8192)
    sqdump = at("sqdump", [128, 512], BF16, offset=O_SCR + 12288)

    uT = at("uT", [128, KC * TOK], BF16, offset=O_UT)
    scores = at("scoresb", [128, SEQ], F32, offset=O_UT)
    pbuf = at("pbuf", [128, SEQ], BF16, offset=O_UT + 16384)
    ptb = at("ptb", [128, SEQ], BF16, offset=O_UT + 24576)

    xsb = at("xsb", [128, NT * D], F32, offset=O_X)
    wqkvT = at("wqkvT", [128, KC * 3 * D], BF16, offset=O_X)
    va = at("va", [128, 16 * D], BF16, offset=O_X)
    vb = at("vb", [128, 16 * D], BF16, offset=O_X + 32768)

    slot_g, slot_u, slot_d = [], [], []
    for s in range(4):
        o = O_W + s * 24576
        slot_g.append(at(f"wg{s}", [128, KC * 512], BF16, offset=o))
        slot_u.append(at(f"wu{s}", [128, KC * 512], BF16, offset=o + 8192))
        slot_d.append(at(f"wd{s}", [128, 4 * D], BF16, offset=o + 16384))

    wstbuf = at("wstbuf", [128, 24 * 1024], BF16, offset=O_W + 49152)
    ktown = at("ktown", [128, KC * TOK], BF16, offset=O_W)
    vown = at("vown", [128, NT * D], BF16, offset=O_W + 32768)
    qT = at("qT", [128, KC * TOK], BF16, offset=O_W + 65536)
    ktf = at("ktf", [128, KC * SEQ], BF16, offset=O_W)

    pb = [nc.alloc_psum_tensor(f"p{i}", [128, 512], F32) for i in range(6)]
    tb = [nc.alloc_psum_tensor(f"t{i}", [128, 1024], BF16) for i in range(2)]
    pbf = [pb[0].bitcast(BF16), pb[1].bitcast(BF16)]
    ptslots = [tb[0][:, 0:512], tb[1][:, 0:512], pbf[0][:, 0:512], pbf[1][:, 0:512]]

    SW = Sem("sw")
    SX = Sem("sx")
    SXO = Sem("sxo")
    SX1 = [Sem("sx1a"), Sem("sx1b")]
    SX2 = Sem("sx2")
    SKV = [Sem(f"skv{i}") for i in range(6)]
    SCCS = [Sem(f"sccs{i}") for i in range(6)]
    SAGK = Sem("sagk")
    SAGV = Sem("sagv")
    SWR = Sem("swr")
    SWB = Sem("swb")
    SXSP = Sem("sxsp")
    SWS = [Sem(f"sws{i}") for i in range(4)]
    SST = [Sem("ssta"), Sem("sstb")]
    SPF = [Sem("spfa"), Sem("spfb")]
    SPE = Sem("spe")
    SDV = Sem("sdv")
    SAC = Sem("sac")
    SCC = Sem("scc")
    SGP = Sem("sgp")
    sems = ([SW, SX, SXO, SX2, SAGK, SAGV, SWR, SWB, SXSP, SPE, SDV, SAC, SCC, SGP]
            + SX1 + SWS + SST + SKV + SCCS + SPF)

    b = Builder()
    M = b.mile

    def dma(queue, out, in_, waits=(), sem=None):
        if sem is None:
            sem = SW if queue == "gp" else SX
        sem.v += 16
        b.op(queue, (lambda h, o=out, i=in_: h.dma_start(out=o, in_=i)),
             waits, ((sem, 16),))
        return (sem, sem.v)

    b.op("gp", lambda h: h.memset(ident[:, :], 1.0), incs=((SGP, 1),))
    SGP.v += 1
    b.op("gp", lambda h: h.memset(sstat[:, 30:31], EPS), incs=((SGP, 1),))
    SGP.v += 1
    b.op("gp", lambda h: h.affine_select(ident[:, :], ident[:, :], [[1, 128]],
                                         ALU.is_equal, 0.0, channel_multiplier=-1),
         [(SGP, 2)], ((SGP, 1),))
    SGP.v += 1
    M["ident"] = SGP.now()

    xin_done = dma("sp", xsb.rearrange("p (t d) -> p t d", d=D),
                   xin.rearrange("(t p) d -> p t d", p=128))

    def chunked(src_ap):
        return src_ap.rearrange("(k p) f -> p k f", p=128)

    def load_unit(u, s, waits, layer):
        if layer == 3 and PARTS == "full":
            row = wcache.ap()[u * 128:(u + 1) * 128, :]
            dma("gp", slot_g[s][:, :], row[:, 0:4096], waits, sem=SWS[s])
            dma("gp", slot_u[s][:, :], row[:, 4096:8192], waits, sem=SWS[s])
            M[f"w{layer}_{u}"] = dma("gp", slot_d[s][:, :], row[:, 8192:12288],
                                     waits, sem=SWS[s])
            return
        if u == 0:
            g, uu, d = sgh.ap()[:, 0:512], suh.ap()[:, 0:512], sdh.ap()[0:512, :]
        elif u == 1:
            g, uu, d = sgh.ap()[:, 512:1024], suh.ap()[:, 512:1024], sdh.ap()[512:1024, :]
        else:
            e = u - 2
            g, uu, d = wgh.ap()[e], wuh.ap()[e], wdh.ap()[e]
        dma("gp", slot_g[s].rearrange("p (k f) -> p k f", k=KC), chunked(g), waits,
            sem=SWS[s])
        dma("gp", slot_u[s].rearrange("p (k f) -> p k f", k=KC), chunked(uu), waits,
            sem=SWS[s])
        M[f"w{layer}_{u}"] = dma("gp", slot_d[s].rearrange("p (k f) -> p k f", k=4),
                                 chunked(d), waits, sem=SWS[s])
        if layer == 1 and PARTS == "full":
            row = wcache.ap()[u * 128:(u + 1) * 128, :]
            wwb = [M[f"w{layer}_{u}"]]
            if u > 0:
                wwb.append(M[f"wb{u - 1}"])
            r1 = dma("sp", row[:, 0:4096], slot_g[s][:, :], wwb, sem=SWB)
            r2 = dma("sp", row[:, 4096:8192], slot_u[s][:, :], wwb + [r1], sem=SWB)
            M[f"wb{u}"] = dma("sp", row[:, 8192:12288], slot_d[s][:, :],
                              wwb + [r2], sem=SWB)

    M["wr"] = dma("gp", wrb.rearrange("p (k e) -> p k e", k=KC), chunked(wrh.ap()),
                  sem=SWR)
    for u in range(4):
        load_unit(u, u, (), 1)

    def emit_ln(tag, x_gate, first_gate=None, tb_gate=None, stats=None):
        sums, sqa, sqb = lnstat[:, 0:16], lnstat[:, 16:32], lnstat[:, 32:48]
        mean, var = lnstat[:, 48:64], lnstat[:, 64:80]
        stdv, rstd, nb = lnstat[:, 80:96], lnstat[:, 96:112], lnstat[:, 112:128]
        if stats is not None:
            sq_done, red_done = stats
        else:
            pair_done = None
            for t in range(NT):
                w = [x_gate[t]] if x_gate[t] is not None else []
                if t == 0 and first_gate is not None:
                    w = w + [first_gate]
                wsq = list(w)
                if pair_done is not None:
                    wsq.append(pair_done)
                SAC.v += 1
                b.op("act", lambda h, t=t: h.activation(
                    pb[2][:, :], xsb[:, t * D:t * D + 512], AF.Square,
                    bias=0.0, scale=1.0, accum_out=sqa[:, t:t + 1]), wsq, ((SAC, 1),))
                SAC.v += 1
                b.op("act", lambda h, t=t: h.activation(
                    pb[3][:, :], xsb[:, t * D + 512:(t + 1) * D], AF.Square,
                    bias=0.0, scale=1.0, accum_out=sqb[:, t:t + 1]), (), ((SAC, 1),))
                pair_done = SAC.now()
                SDV.v += 1
                b.op("dve", lambda h, t=t: h.reduce_sum(
                    sums[:, t:t + 1], xsb[:, t * D:(t + 1) * D], AX.X),
                    w, ((SDV, 1),))
            sq_done, red_done = SAC.now(), SDV.now()

        def dv(emit, waits=()):
            SDV.v += 1
            b.op("dve", emit, waits, ((SDV, 1),))
            return SDV.now()

        h1 = dv(lambda h: h.tensor_add(sqa[:, :], sqa[:, :], sqb[:, :]),
                [sq_done, red_done])
        dv(lambda h: h.tensor_scalar_mul(mean[:, :], sums[:, :], 1.0 / D), [h1])
        h2 = dv(lambda h: h.tensor_scalar_mul(sqa[:, :], sqa[:, :], 1.0 / D))
        h3 = dv(lambda h: h.tensor_mul(var[:, :], mean[:, :], mean[:, :]), [h2])
        h4 = dv(lambda h: h.tensor_sub(var[:, :], sqa[:, :], var[:, :]), [h3])
        SAC.v += 1
        b.op("act", lambda h: h.activation(stdv[:, :], var[:, :], AF.Sqrt,
                                           bias=sstat[:, 30:31], scale=1.0),
             [h4, "ident"], ((SAC, 1),))
        std_done = SAC.now()
        h5 = dv(lambda h: h.reciprocal(rstd[:, :], stdv[:, :]), [std_done])
        nb_done = dv(lambda h: h.scalar_tensor_tensor(
            nb[:, :], mean[:, :], -1.0, rstd[:, :], ALU.mult, ALU.mult), [h5])

        for t in range(NT):
            p = t % 2
            w = [nb_done]
            if stats is not None and x_gate[t] is not None:
                w.append(x_gate[t])
            if t >= 2:
                w.append(M[f"{tag}_utr{t - 2}"])
            SAC.v += 1
            b.op("act", lambda h, t=t, p=p: h.activation(
                u_tile[:, p * D:(p + 1) * D], xsb[:, t * D:(t + 1) * D], AF.Identity,
                bias=nb[:, t:t + 1], scale=rstd[:, t:t + 1]), w, ((SAC, 1),))
            nrm = SAC.now()
            w = [nrm, "ident"]
            if t >= 2:
                w.append(M[f"{tag}_utev{t - 2}"])
            elif tb_gate is not None:
                w.append(tb_gate)

            def tr_emit(h, t=t, p=p):
                for k in range(KC):
                    i = h.transpose(tb[p][:, k * 128:(k + 1) * 128],
                                    u_tile[:, p * D + k * 128:p * D + (k + 1) * 128],
                                    ident[:, :])
                return i
            SPE.v += 1
            b.op("pe", tr_emit, w, ((SPE, 1),))
            M[f"{tag}_utr{t}"] = SPE.now()
            out_ap = uT.rearrange("p (k s) -> p k s", s=TOK)[:, :, t * 128:(t + 1) * 128]
            in_ap = tb[p].rearrange("p (k s) -> p k s", s=128)
            M[f"{tag}_utev{t}"] = dv(
                lambda h, o=out_ap, i=in_ap: h.tensor_copy(o, i),
                [M[f"{tag}_utr{t}"]])

    def emit_router(tag):
        nmx, esum, rcp = sstat[:, 0:1], sstat[:, 1:2], sstat[:, 2:3]
        m1, m2 = sstat[:, 3:4], sstat[:, 4:5]
        for t in range(NT):
            pr = pb[t % 2]

            def mm_emit(h, t=t, pr=pr):
                for k in range(KC):
                    i = h.matmul(pr[:, 0:NE],
                                 uT[:, k * TOK + t * 128:k * TOK + (t + 1) * 128],
                                 wrb[:, k * NE:(k + 1) * NE],
                                 start=(k == 0), stop=(k == KC - 1))
                return i
            w = [M[f"{tag}_utev{t}"], "wr"]
            if t >= 2:
                w.append(M[f"{tag}_rexp{t - 2}"])
            SPE.v += 1
            b.op("pe", mm_emit, w, ((SPE, 1),))
            mm_done = SPE.now()

            def dv(emit, waits=()):
                SDV.v += 1
                b.op("dve", emit, waits, ((SDV, 1),))
                return SDV.now()
            h1 = dv(lambda h, pr=pr: h.reduce_max(nmx, pr[:, 0:NE], AX.X, negate=True),
                    [mm_done])
            SAC.v += 1
            b.op("act", lambda h, pr=pr: h.activation(
                probs[:, :], pr[:, 0:NE], AF.Exp, bias=nmx, scale=1.0,
                accum_out=esum), [h1], ((SAC, 1),))
            M[f"{tag}_rexp{t}"] = SAC.now()
            h2 = dv(lambda h: h.reciprocal(rcp, esum), [SAC.now()])
            h3 = dv(lambda h: h.tensor_scalar_mul(probs[:, :], probs[:, :], rcp), [h2])
            h4 = dv(lambda h: h.reduce_max(m1, probs[:, :], AX.X), [h3])
            h5 = dv(lambda h: h.tensor_scalar(rtmp[:, :], probs[:, :], m1, None,
                                              ALU.is_ge), [h4])
            h6 = dv(lambda h: h.scalar_tensor_tensor(rtmp[:, :], rtmp[:, :], -2.0,
                                                     probs[:, :], ALU.mult, ALU.add), [h5])
            h7 = dv(lambda h: h.reduce_max(m2, rtmp[:, :], AX.X), [h6])
            h8 = dv(lambda h: h.tensor_scalar(rtmp[:, :], probs[:, :], m2, None,
                                              ALU.is_ge), [h7])
            dv(lambda h, t=t: h.scalar_tensor_tensor(
                cw[:, t * NE:(t + 1) * NE], probs[:, :], 0.5, rtmp[:, :],
                ALU.mult, ALU.mult), [h8])
            M[f"{tag}_cw{t}"] = SDV.now()
        M[f"{tag}_router_done"] = SAC.now()
        M[f"{tag}_cw_done"] = SDV.now()

    def emit_units(tag, layer, store=False, out_gate=None, on_final_evac=None,
                   on_group_end=None):
        its = []
        for g in range(NG):
            for t in range(NT):
                for j in range(2):
                    its.append((g, t, j, 2 * g + j))
        NI = len(its)

        def hgmm(i):
            g, t, j, u = its[i]
            s = u % 4
            pg_, pu_ = pb[2 + 2 * (i % 2)], pb[3 + 2 * (i % 2)]
            w = [f"w{layer}_{u}", M[f"{tag}_utev{t}"]]
            if u >= 2:
                w.append(f"{tag}_cw{t}")
            if i >= 2:
                w.append(M[f"{tag}_mul{i - 2}"])

            def emit(h, t=t, s=s, pg_=pg_, pu_=pu_):
                for k in range(KC):
                    h.matmul(pg_[:, :],
                             uT[:, k * TOK + t * 128:k * TOK + (t + 1) * 128],
                             slot_g[s][:, k * 512:(k + 1) * 512],
                             start=(k == 0), stop=(k == KC - 1))
                    i2 = h.matmul(pu_[:, :],
                                  uT[:, k * TOK + t * 128:k * TOK + (t + 1) * 128],
                                  slot_u[s][:, k * 512:(k + 1) * 512],
                                  start=(k == 0), stop=(k == KC - 1))
                return i2
            SPE.v += 1
            b.op("pe", emit, w, ((SPE, 1),))
            M[f"{tag}_hg{i}"] = SPE.now()

        def epi(i):
            g, t, j, u = its[i]
            p = i % 2
            pg_, pu_ = pb[2 + 2 * p], pb[3 + 2 * p]
            w = [M[f"{tag}_hg{i}"]]
            if i >= 2:
                w.append(M[f"{tag}_mul{i - 2}"])
            SAC.v += 1
            b.op("act", lambda h, p=p, pg_=pg_: h.activation(
                sil[:, p * 512:(p + 1) * 512], pg_[:, :], AF.Silu,
                bias=0.0, scale=1.0), w, ((SAC, 1),))
            silu_done = SAC.now()
            scl = 0.5 if u < 2 else cw[:, t * NE + (u - 2):t * NE + (u - 2) + 1]
            w = [silu_done]
            if i >= 2:
                w.append(M[f"{tag}_tr{i - 2}"])
            SDV.v += 1
            b.op("dve", lambda h, p=p, pu_=pu_, scl=scl: h.scalar_tensor_tensor(
                hbuf[:, p * 512:(p + 1) * 512], sil[:, p * 512:(p + 1) * 512],
                scl, pu_[:, :], ALU.mult, ALU.mult), w, ((SDV, 1),))
            M[f"{tag}_mul{i}"] = SDV.now()

        def tr(i):
            p = i % 2
            q3 = i % 3
            w = [M[f"{tag}_mul{i}"]]
            if i >= 2:
                w.append(M[f"{tag}_htev{i - 2}"])
            else:
                w.append(M[f"{tag}_utev{NT - 1}"])

            def emit(h, p=p):
                for k in range(4):
                    i2 = h.transpose(tb[p][:, k * 128:(k + 1) * 128],
                                     hbuf[:, p * 512 + k * 128:p * 512 + (k + 1) * 128],
                                     ident[:, :])
                return i2
            SPE.v += 1
            b.op("pe", emit, w, ((SPE, 1),))
            M[f"{tag}_tr{i}"] = SPE.now()
            w = [M[f"{tag}_tr{i}"]]
            if i >= 3:
                w.append(M[f"{tag}_wd{i - 3}"])
            SDV.v += 1
            b.op("dve", lambda h, q3=q3, p=p: h.tensor_copy(
                hT[:, q3 * 512:(q3 + 1) * 512], tb[p][:, 0:512]), w, ((SDV, 1),))
            M[f"{tag}_htev{i}"] = SDV.now()

        def wd(i):
            g, t, j, u = its[i]
            s = u % 4
            p = i % 3
            w = [M[f"{tag}_htev{i}"]]
            if i < 2:
                w.append(f"{tag}_router_done")
                w.append(f"{tag}_cw_done")
            if j == 0:
                if t > 0:
                    w.append(M[f"{tag}_oev{g}_{t - 1}"])
                elif g > 0:
                    w.append(M[f"{tag}_oev{g - 1}_{NT - 1}"])
                elif out_gate is not None:
                    w.append(out_gate)

            def emit(h, s=s, p=p, j=j):
                for k in range(4):
                    st = (j == 0 and k == 0)
                    sp = (j == 1 and k == 3)
                    h.matmul(pb[0][:, :],
                             hT[:, p * 512 + k * 128:p * 512 + (k + 1) * 128],
                             slot_d[s][:, k * D:k * D + 512], start=st, stop=sp)
                    i2 = h.matmul(pb[1][:, :],
                                  hT[:, p * 512 + k * 128:p * 512 + (k + 1) * 128],
                                  slot_d[s][:, k * D + 512:(k + 1) * D],
                                  start=st, stop=sp)
                return i2
            SPE.v += 1
            b.op("pe", emit, w, ((SPE, 1),))
            M[f"{tag}_wd{i}"] = SPE.now()
            if j == 1:
                M[f"{tag}_wdstop{g}_{t}"] = SPE.now()
                if t == NT - 1 and on_group_end is not None:
                    on_group_end(g)

                def ev(h, t=t):
                    h.tensor_add(xsb[:, t * D:t * D + 512],
                                 xsb[:, t * D:t * D + 512], pb[0][:, :])
                    return h.tensor_add(xsb[:, t * D + 512:(t + 1) * D],
                                        xsb[:, t * D + 512:(t + 1) * D], pb[1][:, :])
                SDV.v += 1
                b.op("dve", ev, [M[f"{tag}_wdstop{g}_{t}"]], ((SDV, 1),))
                M[f"{tag}_oev{g}_{t}"] = SDV.now()
                if on_final_evac is not None and g == NG - 1:
                    on_final_evac(t, [M[f"{tag}_oev{g}_{t}"]])
                if store and g == NG - 1:
                    dma("sp", xout.ap()[t * 128:(t + 1) * 128, :],
                        xsb[:, t * D:(t + 1) * D], [M[f"{tag}_oev{g}_{t}"]],
                        sem=SXO)

        for i in range(NI):
            hgmm(i)
            epi(i)
            if i >= 1:
                tr(i - 1)
            if i >= 2:
                wd(i - 2)
        tr(NI - 1)
        wd(NI - 2)
        wd(NI - 1)
        M[f"{tag}_done"] = SDV.now()

    def make_early_stats(src_fn, dump_ap):
        sums, sqa, sqb = lnstat[:, 0:16], lnstat[:, 16:32], lnstat[:, 32:48]
        state = {"pair": None}

        def emit(t, gate_w):
            src = src_fn(t)
            w = list(gate_w)
            wsq = list(w)
            if state["pair"] is not None:
                wsq.append(state["pair"])
            SAC.v += 1
            b.op("act", lambda h, src=src: h.activation(
                dump_ap, src[:, 0:512], AF.Square,
                bias=0.0, scale=1.0, accum_out=sqa[:, t:t + 1]), wsq, ((SAC, 1),))
            first = SAC.now()
            SAC.v += 1
            b.op("act", lambda h, src=src: h.activation(
                dump_ap, src[:, 512:D], AF.Square,
                bias=0.0, scale=1.0, accum_out=sqb[:, t:t + 1]), [first], ((SAC, 1),))
            state["pair"] = SAC.now()
            SDV.v += 1
            b.op("dve", lambda h, src=src, t=t: h.reduce_sum(
                sums[:, t:t + 1], src[:, :], AX.X), w, ((SDV, 1),))
            state["red"] = SDV.now()

        def done():
            return (state["pair"], state["red"])
        return emit, done

    emit_ln("L1", [xin_done] * NT)
    emit_router("L1")
    if PARTS == "ffn1":
        def l1_groups_f(g):
            if g + 2 >= NG:
                return
            for j in range(2):
                u = 2 * (g + 2) + j
                load_unit(u, u % 4, [f"L1_wdstop{g}_{NT - 1}"], 1)
        emit_units("L1", 1, store=True, on_group_end=l1_groups_f)
    elif True:
        l2st_emit, l2st_done = make_early_stats(
            lambda t: xsb[:, t * D:(t + 1) * D], u_tile[:, 0:512])
        spill_state = {"prev": None}

        def l1_post(t, gate_w):
            l2st_emit(t, gate_w)
            wsp = list(gate_w)
            if spill_state["prev"] is not None:
                wsp.append(spill_state["prev"])
            spill_state["prev"] = dma(
                "sp", xspill.ap()[t * 128:(t + 1) * 128, :],
                xsb[:, t * D:(t + 1) * D], wsp, sem=SXSP)
            M[f"spl{t}"] = spill_state["prev"]
        def l1_groups(g):
            if g + 2 >= NG:
                return
            for j in range(2):
                u = 2 * (g + 2) + j
                wld = [f"L1_wdstop{g}_{NT - 1}"]
                if PARTS == "full" and u >= 4:
                    wld.append(f"wb{u - 4}")
                load_unit(u, u % 4, wld, 1)
        emit_units("L1", 1, store=False, on_final_evac=l1_post,
                   on_group_end=l1_groups)

    if PARTS == "ffn1":
        b.op("sp", None, [(SXO, SXO.v)])
        finish(nc, b, sems)
        return nc

    emit_ln("L2", [None] * NT,
            first_gate="L1_done", tb_gate="L1_done", stats=l2st_done())

    blk_order = list(range(8, 16)) + list(range(16, 24)) + list(range(8))
    for pos, blk in enumerate(blk_order):
        p = pos % 2
        w = ["L1_wdstop7_15"]
        if PARTS == "full":
            w.append("wb15")
        if pos >= 2:
            w.append(M[f"wld{pos - 2}"])
        M[f"wld{pos}"] = dma("gp", wstbuf[:, pos * D:(pos + 1) * D],
                             wqkv.ap()[blk * 128:(blk + 1) * 128, :], w, sem=SST[p])
    for pos, blk in enumerate(blk_order):
        p = pos % 2
        w = [M[f"wld{pos}"], "ident",
             M["L2_utev15"] if pos < 2 else M[f"wstev{blk_order[pos - 2]}"]]

        def tre(h, p=p, pos=pos):
            for k in range(KC):
                i2 = h.transpose(tb[p][:, k * 128:(k + 1) * 128],
                                 wstbuf[:, pos * D + k * 128:pos * D + (k + 1) * 128],
                                 ident[:, :])
            return i2
        SPE.v += 1
        b.op("pe", tre, w, ((SPE, 1),))
        M[f"wst{blk}"] = SPE.now()
        out_ap = wqkvT.rearrange("p (k e) -> p k e", e=3 * D)[:, :, blk * 128:(blk + 1) * 128]
        in_ap = tb[p].rearrange("p (k s) -> p k s", s=128)
        w2 = [M[f"wst{blk}"]]
        if pos == 0:
            w2.append(M["spl11"])
        SDV.v += 1
        b.op("dve", lambda h, o=out_ap, i=in_ap: h.tensor_copy(o, i),
             w2, ((SDV, 1),))
        M[f"wstev{blk}"] = SDV.now()
    wst_all = SDV.now()

    if PARTS == "attn_ln":
        b.op("sp", None, [wst_all, spill_done])
        finish(nc, b, sems)
        return nc

    idx = 0
    for dk in range(KC):
        for tbk in range(4):
            pp = pb[2 + (idx % 2)]
            w = [M[f"wstev{8 + dk}"], M[f"L2_utev{4 * tbk + 3}"]]
            if idx >= 2:
                w.append(M[f"kev{idx - 2}"])

            def kmm(h, dk=dk, tbk=tbk, pp=pp):
                for k in range(KC):
                    i2 = h.matmul(
                        pp[:, :],
                        wqkvT[:, k * 3 * D + D + dk * 128:k * 3 * D + D + (dk + 1) * 128],
                        uT[:, k * TOK + tbk * 512:k * TOK + (tbk + 1) * 512],
                        start=(k == 0), stop=(k == KC - 1))
                return i2
            SPE.v += 1
            b.op("pe", kmm, w, ((SPE, 1),))
            mmd = SPE.now()
            wkev = [mmd]
            if idx < 2 and PARTS == "full":
                wkev.append("wb17")
            SDV.v += 1
            b.op("dve", lambda h, dk=dk, tbk=tbk, pp=pp: h.tensor_copy(
                ktown[:, dk * TOK + tbk * 512:dk * TOK + (tbk + 1) * 512], pp[:, :]),
                wkev, ((SDV, 1),))
            M[f"kev{idx}"] = SDV.now()
            idx += 1
    agk_done = []
    k_staged = []
    for hh in range(4):
        wks = [M[f"kev{7 + 8 * hh}"]] + ([k_staged[-1]] if hh > 0 else [])
        ks = dma("sp", agk_in[hh].ap(), ktown[:, hh * 4096:(hh + 1) * 4096],
                 wks, sem=SAGK)
        k_staged.append(ks)
        if FAKE_CC:
            dma("gp", agk_out[hh].ap()[0:128, :], agk_in[hh].ap(), [ks], sem=SW)
            agk_done.append(dma("gp", agk_out[hh].ap()[128:256, :], agk_in[hh].ap(),
                                [ks], sem=SW))
        else:
            SCCS[hh].v += 1
            b.op("gp", lambda h, hh=hh: h.collective_compute(
                "AllGather", ALU.bypass, replica_groups=RG,
                ins=[agk_in[hh].ap().opt()], outs=[agk_out[hh].ap().opt()]),
                [ks], ((SCCS[hh], 1),))
            agk_done.append(SCCS[hh].now())

    for t in range(NT):
        pA, pB = (pb[2], pb[3]) if t % 2 == 0 else (pb[4], pb[5])
        w = [M["wstev23"]]
        w += [M["kev31"]] if t < 2 else [M[f"vev{t - 2}"]]

        def vmm(h, t=t, pA=pA, pB=pB):
            for k in range(KC):
                h.matmul(pA[:, :], uT[:, k * TOK + t * 128:k * TOK + (t + 1) * 128],
                         wqkvT[:, k * 3 * D + 2 * D:k * 3 * D + 2 * D + 512],
                         start=(k == 0), stop=(k == KC - 1))
                i2 = h.matmul(pB[:, :], uT[:, k * TOK + t * 128:k * TOK + (t + 1) * 128],
                              wqkvT[:, k * 3 * D + 2 * D + 512:k * 3 * D + 3 * D],
                              start=(k == 0), stop=(k == KC - 1))
            return i2
        SPE.v += 1
        b.op("pe", vmm, w, ((SPE, 1),))
        mmd = SPE.now()

        def vev(h, t=t, pA=pA, pB=pB):
            h.tensor_copy(vown[:, t * D:t * D + 512], pA[:, :])
            return h.tensor_copy(vown[:, t * D + 512:(t + 1) * D], pB[:, :])
        wvev = [mmd]
        if t == 0 and PARTS == "full":
            wvev.append("wb17")
        SDV.v += 1
        b.op("dve", vev, wvev, ((SDV, 1),))
        M[f"vev{t}"] = SDV.now()
    agv_done = []
    v_staged = []
    for hh in range(2):
        wvs = [M[f"vev{7 + 8 * hh}"]] + ([v_staged[0]] if hh == 1 else [])
        vs = dma("sp", agv_in[hh].ap(), vown[:, hh * 8192:(hh + 1) * 8192],
                 wvs, sem=SAGV)
        v_staged.append(vs)
        if FAKE_CC:
            dma("gp", agv_out[hh].ap()[0:128, :], agv_in[hh].ap(), [vs], sem=SW)
            agv_done.append(dma("gp", agv_out[hh].ap()[128:256, :], agv_in[hh].ap(),
                                [vs], sem=SW))
        else:
            SCCS[4 + hh].v += 1
            b.op("gp", lambda h, hh=hh: h.collective_compute(
                "AllGather", ALU.bypass, replica_groups=RG,
                ins=[agv_in[hh].ap().opt()], outs=[agv_out[hh].ap().opt()]),
                [vs], ((SCCS[4 + hh], 1),))
            agv_done.append(SCCS[4 + hh].now())

    idx = 0
    for tbk in range(4):
        for dq in range(KC):
            pp = pb[2 + (idx % 2)]
            w = [M[f"wstev{dq}"]]
            w += [M[f"vev{14 + idx}"]] if idx < 2 else [M[f"qev{idx - 2}"]]

            def qmm(h, dq=dq, tbk=tbk, pp=pp):
                for k in range(KC):
                    i2 = h.matmul(
                        pp[:, :],
                        wqkvT[:, k * 3 * D + dq * 128:k * 3 * D + (dq + 1) * 128],
                        uT[:, k * TOK + tbk * 512:k * TOK + (tbk + 1) * 512],
                        start=(k == 0), stop=(k == KC - 1))
                return i2
            SPE.v += 1
            b.op("pe", qmm, w, ((SPE, 1),))
            mmd = SPE.now()
            SDV.v += 1
            b.op("dve", lambda h, dq=dq, tbk=tbk, pp=pp: h.tensor_scalar_mul(
                qT[:, dq * TOK + tbk * 512:dq * TOK + (tbk + 1) * 512],
                pp[:, :], 1.0 / 32.0), [mmd], ((SDV, 1),))
            M[f"qev{tbk * 8 + dq}"] = SDV.now()
            idx += 1
    q_done = SDV.now()
    M["qproj_pe"] = SPE.now()

    ktf_v = ktf.rearrange("p (k s) -> p k s", s=SEQ)
    for hh in range(4):
        ko_v0 = agk_out[hh].ap()[0:128, :].rearrange("p (k s) -> p k s", s=TOK)
        ko_v1 = agk_out[hh].ap()[128:256, :].rearrange("p (k s) -> p k s", s=TOK)
        wv = [agk_done[hh], k_staged[3]] if hh < 2 else [agk_done[hh], v_staged[1]]
        r0 = dma("sp", ktf_v[:, 2 * hh:2 * hh + 2, 0:TOK], ko_v0, wv, sem=SKV[hh])
        r = dma("sp", ktf_v[:, 2 * hh:2 * hh + 2, TOK:SEQ], ko_v1, wv + [r0],
                sem=SKV[hh])
        M[f"ktfQ{hh}"] = r
    for hh in range(2):
        wv = [agv_done[hh], "qproj_pe", "spl15"]
        r0 = dma("sp", va[:, hh * 8192:(hh + 1) * 8192], agv_out[hh].ap()[0:128, :],
                 wv, sem=SKV[4 + hh])
        r = dma("sp", vb[:, hh * 8192:(hh + 1) * 8192], agv_out[hh].ap()[128:256, :],
                wv + [r0], sem=SKV[4 + hh])
        if hh == 0:
            M["vA"] = r
        else:
            M["vB"] = r

    if PARTS == "attn_qkv":
        b.op("sp", None, [kv_loaded, q_done])
        finish(nc, b, sems)
        return nc

    if PARTS == "full":
        l3st_emit, l3st_done = make_early_stats(lambda t: x2st[:, :], sqdump[:, :])

    pmax, psums = sstat[:, 8:16], sstat[:, 16:24]
    nmx_a, ssum, rinv = sstat[:, 24:25], sstat[:, 25:26], sstat[:, 26:27]

    def attn_scores(q, half):
        if True:
            for dc in range(KC):
                w = []
                if q == 0 and half == 0 and dc == 0:
                    w = [M["ktfQ0"], q_done]
                if q == 0 and half == 0 and dc in (2, 4, 6):
                    w = [M[f"ktfQ{dc // 2}"]]
                if dc == 0 and (q > 0 or half > 0):
                    pq_, ph_ = (q, 0) if half == 1 else (q - 1, 1)
                    w.append(M[f"sev{pq_}_{ph_}"])

                def smm(h, q=q, half=half, dc=dc):
                    for ktl in range(4):
                        i2 = h.matmul(
                            pb[2 + ktl][:, :],
                            qT[:, dc * TOK + q * 128:dc * TOK + (q + 1) * 128],
                            ktf[:, dc * SEQ + (half * 4 + ktl) * 512:
                                dc * SEQ + (half * 4 + ktl + 1) * 512],
                            start=(dc == 0), stop=(dc == KC - 1))
                    return i2
                if dc == KC - 1:
                    SPE.v += 1
                    b.op("pe", smm, w, ((SPE, 1),))
                else:
                    b.op("pe", smm, w)
            stop_m = SPE.now()
            w = [stop_m]
            if q > 0:
                w.append(f"nmx{q - 1}")
                w.append(M[f"exp{q - 1}_7"])

            for ktl in range(4):
                kt = half * 4 + ktl
                SAC.v += 1
                b.op("act", lambda h, kt=kt, ktl=ktl: h.activation(
                    scores[:, kt * 512:(kt + 1) * 512],
                    pb[2 + ktl][:, :], AF.Copy, bias=0.0, scale=1.0),
                    w if ktl == 0 else (), ((SAC, 1),))
                M[f"sev{q}_{half}_{ktl}"] = SAC.now()
            M[f"sev{q}_{half}"] = SAC.now()

    def attn_pmax(q, half):
        def pm(h, half=half):
            for kt in range(4 * half, 4 * half + 4):
                i2 = h.reduce_max(pmax[:, kt:kt + 1],
                                  scores[:, kt * 512:(kt + 1) * 512], AX.X)
            return i2
        SDV.v += 1
        b.op("dve", pm, [M[f"sev{q}_{half}"]], ((SDV, 1),))
        M[f"pmx{q}_{half}"] = SDV.now()

    def attn_softmax_part(q):
        def dv(emit, waits=()):
            SDV.v += 1
            b.op("dve", emit, waits, ((SDV, 1),))
            return SDV.now()

        M[f"nmx{q}"] = dv(lambda h: h.reduce_max(nmx_a, pmax[:, :], AX.X, negate=True),
                          [M[f"pmx{q}_0"], M[f"pmx{q}_1"]])
        for kt in range(8):
            w = [M[f"nmx{q}"]]
            if kt == 0 and q > 0:
                w.append(M[f"ptr{q - 1}"])
                w.append(M[f"rinv{q - 1}"])
            SAC.v += 1
            b.op("act", lambda h, kt=kt: h.activation(
                pbuf[:, kt * 512:(kt + 1) * 512], scores[:, kt * 512:(kt + 1) * 512],
                AF.Exp, bias=nmx_a, scale=1.0), w, ((SAC, 1),))
            M[f"exp{q}_{kt}"] = SAC.now()



    def attn_ptpv_part(q):
        def dv(emit, waits=()):
            SDV.v += 1
            b.op("dve", emit, waits, ((SDV, 1),))
            return SDV.now()

        banks = [tb[0], tb[1], pbf[0], pbf[1]]
        for bk in range(4):
            bank = banks[bk]
            for half2 in range(2):
                jj = 2 * bk + half2
                w = [M[f"exp{q}_{jj}"], "ident"]
                if half2 == 0:
                    if q > 0:
                        w.append(M[f"pteb{q - 1}_{bk}"])
                    else:
                        w.append(M["qev7"])
                    if bk >= 2 and q > 0:
                        w.append(M[f"oev{q - 1}"])

                def ptr(h, jj=jj, bank=bank, half2=half2):
                    for m in range(4):
                        i2 = h.transpose(
                            bank[:, half2 * 512 + m * 128:half2 * 512 + (m + 1) * 128],
                            pbuf[:, (4 * jj + m) * 128:(4 * jj + m + 1) * 128],
                            ident[:, :])
                    return i2
                SPE.v += 1
                b.op("pe", ptr, w, ((SPE, 1),))
                trd = SPE.now()
            if bk == 3:
                M[f"ptr{q}"] = trd
            w = [trd]
            if q > 0:
                w.append(M[f"pv{q - 1}"])
            dv(lambda h, bk=bk, bank=bank: h.tensor_copy(
                ptb[:, bk * 1024:(bk + 1) * 1024], bank[:, :]), w)
            M[f"pteb{q}_{bk}"] = SDV.now()
        pte_all = SDV.now()
        pte_all_a = None
        if PARTS == "s_pt":
            return

        w = [pte_all]
        if q > 0:
            w.append(M[f"oev{q - 1}"])

        order = list(range(0, 8)) + list(range(16, 24)) + \
            list(range(8, 16)) + list(range(24, 32))

        def pv1(h, q=q):
            for n2, kc2 in enumerate(order[:16]):
                vsrc = va if kc2 < 16 else vb
                off = (kc2 % 16) * D
                h.matmul(pb[0][:, :], ptb[:, kc2 * 128:(kc2 + 1) * 128],
                         vsrc[:, off:off + 512], start=(n2 == 0), stop=False)
                i2 = h.matmul(pb[1][:, :], ptb[:, kc2 * 128:(kc2 + 1) * 128],
                              vsrc[:, off + 512:off + D], start=(n2 == 0), stop=False)
            return i2

        def pv2(h, q=q):
            for n2, kc2 in enumerate(order[16:]):
                vsrc = va if kc2 < 16 else vb
                off = (kc2 % 16) * D
                h.matmul(pb[0][:, :], ptb[:, kc2 * 128:(kc2 + 1) * 128],
                         vsrc[:, off:off + 512], start=False, stop=(n2 == 15))
                i2 = h.matmul(pb[1][:, :], ptb[:, kc2 * 128:(kc2 + 1) * 128],
                              vsrc[:, off + 512:off + D], start=False, stop=(n2 == 15))
            return i2
        if q == 0:
            w.append(M["vA"])
        b.op("pe", pv1, w)
        w2 = [M["vB"]] if q == 0 else []
        SPE.v += 1
        b.op("pe", pv2, w2, ((SPE, 1),))
        M[f"pv{q}"] = SPE.now()

    def attn_oev(q):
        p = q % 2
        SDV.v += 1
        b.op("dve", lambda h: h.reduce_sum(ssum, pbuf[:, :], AX.X),
             [M[f"exp{q}_7"]], ((SDV, 1),))
        h2 = SDV.now()
        SDV.v += 1
        b.op("dve", lambda h: h.reciprocal(rinv, ssum), [h2], ((SDV, 1),))
        M[f"rinv{q}"] = SDV.now()
        w = [M[f"pv{q}"], M[f"x1l{q}"], M[f"rinv{q}"]]
        if q >= 1:
            w.append(M[f"x2sp{q - 1}"])
            if PARTS == "full":
                w.append(M[f"l3sq{q - 1}"])
            if PARTS == "attn":
                w.append(M[f"xo{q - 1}"])

        def ev(h, p=p):
            h.scalar_tensor_tensor(x2st[:, 0:512], pb[0][:, :], rinv,
                                   x1s[:, p * D:p * D + 512], ALU.mult, ALU.add)
            return h.scalar_tensor_tensor(x2st[:, 512:1024], pb[1][:, :], rinv,
                                          x1s[:, p * D + 512:(p + 1) * D],
                                          ALU.mult, ALU.add)
        SDV.v += 1
        b.op("dve", ev, w, ((SDV, 1),))
        M[f"oev{q}"] = SDV.now()
        if PARTS == "full":
            l3st_emit(q, [M[f"oev{q}"]])
            M[f"l3sq{q}"] = (SAC, SAC.v)
        if PARTS == "attn":
            M[f"xo{q}"] = dma("sp", xout.ap()[q * 128:(q + 1) * 128, :], x2st[:, :],
                              [M[f"oev{q}"]], sem=SXO)
        M[f"x2sp{q}"] = dma("sp", x2spill.ap()[q * 128:(q + 1) * 128, :],
                            x2st[:, :], [M[f"oev{q}"]], sem=SX2)

    if PARTS in ("s_sc", "s_sm", "s_pt", "s_pv"):
        attn_scores(0, 0)
        attn_pmax(0, 0)
        attn_scores(0, 1)
        attn_pmax(0, 1)
        M["x1l0"] = dma("sp", x1s[:, 0:D], xspill.ap()[0:128, :],
                        [M["spl0"], "L2_utev15"], sem=SX1[0])
        if PARTS == "s_sc":
            b.op("sp", None, [M["sev0_1"]])
            finish(nc, b, sems); return nc
        attn_softmax_part(0)
        if PARTS == "s_sm":
            b.op("sp", None, [M["rinv0"]])
            finish(nc, b, sems); return nc
        attn_ptpv_part(0)
        if PARTS == "s_pt":
            b.op("sp", None, [M["pte0_7"] if "pte0_7" in M else M["pv0"]])
            finish(nc, b, sems); return nc
        attn_oev(0)
        b.op("sp", None, [(SX2, SX2.v)])
        finish(nc, b, sems); return nc

    NQ = NT if NQLIM is None else NQLIM
    for q in range(NQ):
        if q > 0:
            attn_softmax_part(q - 1)
        attn_scores(q, 0)
        w = [M[f"spl{q}"], "L2_utev15"] if q < 2 else [f"oev{q - 2}"]
        M[f"x1l{q}"] = dma("sp", x1s[:, (q % 2) * D:(q % 2 + 1) * D],
                           xspill.ap()[q * 128:(q + 1) * 128, :], w, sem=SX1[q % 2])
        if q > 0:
            attn_ptpv_part(q - 1)
        attn_pmax(q, 0)
        attn_scores(q, 1)
        attn_pmax(q, 1)
        if q > 0:
            attn_oev(q - 1)
    attn_softmax_part(NQ - 1)
    attn_ptpv_part(NQ - 1)
    attn_oev(NQ - 1)
    M["L2_dve_end"] = SDV.now()
    if NQLIM is not None:
        b.op("sp", None, [(SX2, SX2.v)])
        finish(nc, b, sems)
        return nc

    if PARTS == "attn":
        b.op("sp", None, [(SXO, SXO.v), (SX2, SX2.v)])
        finish(nc, b, sems)
        return nc

    pv_last = M[f"pv{NT - 1}"]
    xin3_parts = []
    for c4 in range(8):
        wpre = [pv_last, M[f"x2sp{2 * c4 + 1}"]]
        if c4 >= 2:
            wpre.append(xin3_parts[c4 - 2])
        xin3_parts.append(dma(
            "sp",
            xsb.rearrange("p (t d) -> p t d", d=D)[:, c4 * 2:(c4 + 1) * 2, :],
            x2spill.rearrange("(t p) d -> p t d", p=128)[:, c4 * 2:(c4 + 1) * 2, :],
            wpre, sem=SPF[c4 % 2]))
    xin3_done = xin3_parts[-1]
    for u in range(4):
        load_unit(u, u, [M["sev15_1"]], 3)

    emit_ln("L3", [xin3_parts[t // 2] for t in range(NT)],
            first_gate="L2_dve_end", tb_gate="L2_dve_end", stats=l3st_done())
    emit_router("L3")
    emit_units("L3", 3, store=True, out_gate=f"oev{NT - 1}")
    for g in range(2, NG):
        for j in range(2):
            u = 2 * g + j
            load_unit(u, u % 4, [f"L3_wdstop{g - 2}_{NT - 1}"], 3)

    b.op("sp", None, [(SXO, SXO.v)])
    finish(nc, b, sems)
    return nc


def finish(nc, b, sems):
    from contextlib import ExitStack
    ctx = ExitStack()
    for s in sems:
        s.h = ctx.enter_context(nc.semaphore(s.name))
    eng_map = {"sp": "sync", "gp": "gpsimd", "pe": "tensor", "dve": "vector",
               "act": "scalar"}
    with nc.Block() as block:
        for ename, attr in eng_map.items():
            def run(h, ename=ename):
                last = {}
                for (e, waits, emit, incs) in b.ops:
                    if e != ename:
                        continue
                    for wref in waits:
                        w = b.resolve(wref)
                        if w is None:
                            continue
                        s, v = w
                        if v > 0 and last.get(s, -1) < v:
                            h.wait_ge(s.h, v)
                            last[s] = v
                    if emit is None:
                        continue
                    inst = emit(h)
                    for (s, amt) in incs:
                        inst.then_inc(s.h, amt)
            getattr(block, attr)(run)
    ctx.close()


def kernel(L0_attn_x, L0_attn_W_qkv, L0_ffn_Wr, L0_ffn_Wg, L0_ffn_Wu, L0_ffn_Wd,
           L0_ffn_Sg, L0_ffn_Su, L0_ffn_Sd):
    x = np.ascontiguousarray(np.asarray(L0_attn_x, dtype=np.float32)).reshape(8, TOK, D)
    com = {
        "wqkv": np.ascontiguousarray(np.asarray(L0_attn_W_qkv, dtype=np.float32)),
        "wr": np.ascontiguousarray(np.asarray(L0_ffn_Wr, dtype=np.float32)),
        "wgh": np.ascontiguousarray(np.asarray(L0_ffn_Wg, dtype=np.float32)),
        "wuh": np.ascontiguousarray(np.asarray(L0_ffn_Wu, dtype=np.float32)),
        "wdh": np.ascontiguousarray(np.asarray(L0_ffn_Wd, dtype=np.float32)),
        "sgh": np.ascontiguousarray(np.asarray(L0_ffn_Sg, dtype=np.float32)),
        "suh": np.ascontiguousarray(np.asarray(L0_ffn_Su, dtype=np.float32)),
        "sdh": np.ascontiguousarray(np.asarray(L0_ffn_Sd, dtype=np.float32)),
    }
    nc = build_program()
    in_maps = [dict(com, xin=np.ascontiguousarray(x[c])) for c in range(8)]
    res = run_bass_kernel_spmd(nc, in_maps, core_ids=list(range(8)))
    out = np.concatenate([res.results[c]["xout"][None] for c in range(8)], axis=0)
    return out.reshape(4, 4096, D)


# version 09
# speedup vs baseline: 1.0410x; 1.0410x over previous
import numpy as np
import concourse.bass as bass
import concourse.mybir as mybir
from concourse.bass_utils import run_bass_kernel_spmd

F32 = mybir.dt.float32
BF16 = mybir.dt.bfloat16
AF = mybir.ActivationFunctionType
ALU = mybir.AluOpType
AX = mybir.AxisListType

D = 1024
TOK = 2048
NT = 16
KC = 8
NE = 16
NU = 18
NG = NU // 2
SEQ = 4096
EPS = 1e-5
RG = [[0, 1], [2, 3], [4, 5], [6, 7]]

PARTS = "full"
FAKE_CC = False
NQLIM = None

BASE = 16512
O_SCR = BASE + 2304
O_WRB = O_SCR + 13312
O_UT = O_SCR + 13568
O_X = O_UT + 32768
O_W = O_X + 65536


class Sem:
    def __init__(self, name):
        self.name = name
        self.v = 0
        self.h = None

    def now(self):
        return (self, self.v)


class Builder:
    def __init__(self):
        self.ops = []
        self.mile = {}

    def op(self, eng, emit, waits=(), incs=()):
        self.ops.append((eng, tuple(waits), emit, tuple(incs)))

    def resolve(self, w):
        if isinstance(w, str):
            return self.mile[w]
        return w


def build_program():
    nc = bass.Bass("TRN2", target_bir_lowering=False)

    xin = nc.dram_tensor("xin", [TOK, D], F32, kind="ExternalInput")
    wqkv = nc.dram_tensor("wqkv", [3 * D, D], F32, kind="ExternalInput")
    wrh = nc.dram_tensor("wr", [D, NE], F32, kind="ExternalInput")
    wgh = nc.dram_tensor("wgh", [NE, D, 512], F32, kind="ExternalInput")
    wuh = nc.dram_tensor("wuh", [NE, D, 512], F32, kind="ExternalInput")
    wdh = nc.dram_tensor("wdh", [NE, 512, D], F32, kind="ExternalInput")
    sgh = nc.dram_tensor("sgh", [D, D], F32, kind="ExternalInput")
    suh = nc.dram_tensor("suh", [D, D], F32, kind="ExternalInput")
    sdh = nc.dram_tensor("sdh", [D, D], F32, kind="ExternalInput")
    xout = nc.dram_tensor("xout", [TOK, D], F32, kind="ExternalOutput")

    xspill = nc.dram_tensor("xspill", [TOK, D], F32, kind="Internal")
    wcache = nc.dram_tensor("wcache", [NU * 128, 12288], BF16, kind="Internal")
    x2spill = nc.dram_tensor("x2spill", [TOK, D], F32, kind="Internal")
    agk_in = [nc.dram_tensor(f"agk_in{i}", [128, KC * TOK // 4], BF16, kind="Internal")
              for i in range(4)]
    agk_out = [nc.dram_tensor(f"agk_out{i}", [256, KC * TOK // 4], BF16, kind="Internal")
               for i in range(4)]
    agv_in = [nc.dram_tensor(f"agv_in{i}", [128, NT * D // 2], BF16, kind="Internal")
              for i in range(2)]
    agv_out = [nc.dram_tensor(f"agv_out{i}", [256, NT * D // 2], BF16, kind="Internal")
               for i in range(2)]

    at = nc.alloc_sbuf_tensor_at
    ident = at("ident", [128, 128], BF16, offset=BASE)
    lnstat = at("lnstat", [128, 128], F32, offset=BASE + 256)
    sstat = at("sstat", [128, 32], F32, offset=BASE + 768)
    probs = at("probs", [128, NE], F32, offset=BASE + 896)
    rtmp = at("rtmp", [128, NE], F32, offset=BASE + 960)
    cw = at("cw", [128, NT * NE], F32, offset=BASE + 1024)
    wrb = at("wrb", [128, KC * NE], BF16, offset=O_WRB)

    u_tile = at("u_tile", [128, 2 * D], BF16, offset=O_SCR)
    sil = at("sil", [128, 2 * 512], F32, offset=O_SCR + 4096)
    wstage = at("wstage", [128, 2 * D], BF16, offset=O_SCR + 4096)
    hbuf = at("hbuf", [128, 2 * 512], BF16, offset=O_SCR + 8192)
    hT = at("hT", [128, 3 * 512], BF16, offset=O_SCR + 10240)
    x1s = at("x1s", [128, 2 * D], F32, offset=O_SCR)
    x2st = at("x2st", [128, D], F32, offset=O_SCR + 8192)
    sqdump = at("sqdump", [128, 512], BF16, offset=O_SCR + 12288)

    uT = at("uT", [128, KC * TOK], BF16, offset=O_UT)
    scores = at("scoresb", [128, SEQ], F32, offset=O_UT)
    pbuf = at("pbuf", [128, SEQ], BF16, offset=O_UT + 16384)
    ptb = at("ptb", [128, SEQ], BF16, offset=O_UT + 24576)

    xsb = at("xsb", [128, NT * D], F32, offset=O_X)
    wqkvT = at("wqkvT", [128, KC * 3 * D], BF16, offset=O_X)
    va = at("va", [128, 16 * D], BF16, offset=O_X)
    vb = at("vb", [128, 16 * D], BF16, offset=O_X + 32768)

    slot_g, slot_u, slot_d = [], [], []
    for s in range(4):
        o = O_W + s * 24576
        slot_g.append(at(f"wg{s}", [128, KC * 512], BF16, offset=o))
        slot_u.append(at(f"wu{s}", [128, KC * 512], BF16, offset=o + 8192))
        slot_d.append(at(f"wd{s}", [128, 4 * D], BF16, offset=o + 16384))

    wstbuf = at("wstbuf", [128, 24 * 1024], BF16, offset=O_W + 49152)
    ktown = at("ktown", [128, KC * TOK], BF16, offset=O_W)
    vown = at("vown", [128, NT * D], BF16, offset=O_W + 32768)
    qT = at("qT", [128, KC * TOK], BF16, offset=O_W + 65536)
    ktf = at("ktf", [128, KC * SEQ], BF16, offset=O_W)

    pb = [nc.alloc_psum_tensor(f"p{i}", [128, 512], F32) for i in range(6)]
    tb = [nc.alloc_psum_tensor(f"t{i}", [128, 1024], BF16) for i in range(2)]
    pbf = [pb[0].bitcast(BF16), pb[1].bitcast(BF16)]
    ptslots = [tb[0][:, 0:512], tb[1][:, 0:512], pbf[0][:, 0:512], pbf[1][:, 0:512]]

    SW = Sem("sw")
    SX = Sem("sx")
    SXO = Sem("sxo")
    SX1 = [Sem("sx1a"), Sem("sx1b")]
    SX2 = Sem("sx2")
    SKV = [Sem(f"skv{i}") for i in range(6)]
    SCCS = [Sem(f"sccs{i}") for i in range(6)]
    SAGK = Sem("sagk")
    SAGV = Sem("sagv")
    SWR = Sem("swr")
    SWB = Sem("swb")
    SXSP = Sem("sxsp")
    SWS = [Sem(f"sws{i}") for i in range(4)]
    SST = [Sem("ssta"), Sem("sstb")]
    SPF = [Sem("spfa"), Sem("spfb")]
    SPE = Sem("spe")
    SDV = Sem("sdv")
    SAC = Sem("sac")
    SCC = Sem("scc")
    SGP = Sem("sgp")
    sems = ([SW, SX, SXO, SX2, SAGK, SAGV, SWR, SWB, SXSP, SPE, SDV, SAC, SCC, SGP]
            + SX1 + SWS + SST + SKV + SCCS + SPF)

    b = Builder()
    M = b.mile

    def dma(queue, out, in_, waits=(), sem=None):
        if sem is None:
            sem = SW if queue == "gp" else SX
        sem.v += 16
        b.op(queue, (lambda h, o=out, i=in_: h.dma_start(out=o, in_=i)),
             waits, ((sem, 16),))
        return (sem, sem.v)

    b.op("gp", lambda h: h.memset(ident[:, :], 1.0), incs=((SGP, 1),))
    SGP.v += 1
    b.op("gp", lambda h: h.memset(sstat[:, 30:31], EPS), incs=((SGP, 1),))
    SGP.v += 1
    b.op("gp", lambda h: h.affine_select(ident[:, :], ident[:, :], [[1, 128]],
                                         ALU.is_equal, 0.0, channel_multiplier=-1),
         [(SGP, 2)], ((SGP, 1),))
    SGP.v += 1
    M["ident"] = SGP.now()

    xin_done = dma("sp", xsb.rearrange("p (t d) -> p t d", d=D),
                   xin.rearrange("(t p) d -> p t d", p=128))

    def chunked(src_ap):
        return src_ap.rearrange("(k p) f -> p k f", p=128)

    def load_unit(u, s, waits, layer):
        if layer == 3 and PARTS == "full":
            row = wcache.ap()[u * 128:(u + 1) * 128, :]
            dma("gp", slot_g[s][:, :], row[:, 0:4096], waits, sem=SWS[s])
            dma("gp", slot_u[s][:, :], row[:, 4096:8192], waits, sem=SWS[s])
            M[f"w{layer}_{u}"] = dma("gp", slot_d[s][:, :], row[:, 8192:12288],
                                     waits, sem=SWS[s])
            return
        if u == 0:
            g, uu, d = sgh.ap()[:, 0:512], suh.ap()[:, 0:512], sdh.ap()[0:512, :]
        elif u == 1:
            g, uu, d = sgh.ap()[:, 512:1024], suh.ap()[:, 512:1024], sdh.ap()[512:1024, :]
        else:
            e = u - 2
            g, uu, d = wgh.ap()[e], wuh.ap()[e], wdh.ap()[e]
        dma("gp", slot_g[s].rearrange("p (k f) -> p k f", k=KC), chunked(g), waits,
            sem=SWS[s])
        dma("gp", slot_u[s].rearrange("p (k f) -> p k f", k=KC), chunked(uu), waits,
            sem=SWS[s])
        M[f"w{layer}_{u}"] = dma("gp", slot_d[s].rearrange("p (k f) -> p k f", k=4),
                                 chunked(d), waits, sem=SWS[s])
        if layer == 1 and PARTS == "full":
            row = wcache.ap()[u * 128:(u + 1) * 128, :]
            wwb = [M[f"w{layer}_{u}"]]
            if u > 0:
                wwb.append(M[f"wb{u - 1}"])
            r1 = dma("sp", row[:, 0:4096], slot_g[s][:, :], wwb, sem=SWB)
            r2 = dma("sp", row[:, 4096:8192], slot_u[s][:, :], wwb + [r1], sem=SWB)
            M[f"wb{u}"] = dma("sp", row[:, 8192:12288], slot_d[s][:, :],
                              wwb + [r2], sem=SWB)

    M["wr"] = dma("gp", wrb.rearrange("p (k e) -> p k e", k=KC), chunked(wrh.ap()),
                  sem=SWR)
    for u in range(4):
        load_unit(u, u, (), 1)

    def emit_ln(tag, x_gate, first_gate=None, tb_gate=None, stats=None):
        sums, sqa, sqb = lnstat[:, 0:16], lnstat[:, 16:32], lnstat[:, 32:48]
        mean, var = lnstat[:, 48:64], lnstat[:, 64:80]
        stdv, rstd, nb = lnstat[:, 80:96], lnstat[:, 96:112], lnstat[:, 112:128]
        if stats is not None:
            sq_done, red_done = stats
        else:
            pair_done = None
            for t in range(NT):
                w = [x_gate[t]] if x_gate[t] is not None else []
                if t == 0 and first_gate is not None:
                    w = w + [first_gate]
                wsq = list(w)
                if pair_done is not None:
                    wsq.append(pair_done)
                SAC.v += 1
                b.op("act", lambda h, t=t: h.activation(
                    pb[2][:, :], xsb[:, t * D:t * D + 512], AF.Square,
                    bias=0.0, scale=1.0, accum_out=sqa[:, t:t + 1]), wsq, ((SAC, 1),))
                SAC.v += 1
                b.op("act", lambda h, t=t: h.activation(
                    pb[3][:, :], xsb[:, t * D + 512:(t + 1) * D], AF.Square,
                    bias=0.0, scale=1.0, accum_out=sqb[:, t:t + 1]), (), ((SAC, 1),))
                pair_done = SAC.now()
                SDV.v += 1
                b.op("dve", lambda h, t=t: h.reduce_sum(
                    sums[:, t:t + 1], xsb[:, t * D:(t + 1) * D], AX.X),
                    w, ((SDV, 1),))
            sq_done, red_done = SAC.now(), SDV.now()

        def dv(emit, waits=()):
            SDV.v += 1
            b.op("dve", emit, waits, ((SDV, 1),))
            return SDV.now()

        h1 = dv(lambda h: h.tensor_add(sqa[:, :], sqa[:, :], sqb[:, :]),
                [sq_done, red_done])
        dv(lambda h: h.tensor_scalar_mul(mean[:, :], sums[:, :], 1.0 / D), [h1])
        h2 = dv(lambda h: h.tensor_scalar_mul(sqa[:, :], sqa[:, :], 1.0 / D))
        h3 = dv(lambda h: h.tensor_mul(var[:, :], mean[:, :], mean[:, :]), [h2])
        h4 = dv(lambda h: h.tensor_sub(var[:, :], sqa[:, :], var[:, :]), [h3])
        SAC.v += 1
        b.op("act", lambda h: h.activation(stdv[:, :], var[:, :], AF.Sqrt,
                                           bias=sstat[:, 30:31], scale=1.0),
             [h4, "ident"], ((SAC, 1),))
        std_done = SAC.now()
        h5 = dv(lambda h: h.reciprocal(rstd[:, :], stdv[:, :]), [std_done])
        nb_done = dv(lambda h: h.scalar_tensor_tensor(
            nb[:, :], mean[:, :], -1.0, rstd[:, :], ALU.mult, ALU.mult), [h5])

        for t in range(NT):
            p = t % 2
            w = [nb_done]
            if stats is not None and x_gate[t] is not None:
                w.append(x_gate[t])
            if t >= 2:
                w.append(M[f"{tag}_utr{t - 2}"])
            SAC.v += 1
            b.op("act", lambda h, t=t, p=p: h.activation(
                u_tile[:, p * D:(p + 1) * D], xsb[:, t * D:(t + 1) * D], AF.Identity,
                bias=nb[:, t:t + 1], scale=rstd[:, t:t + 1]), w, ((SAC, 1),))
            nrm = SAC.now()
            w = [nrm, "ident"]
            if t >= 2:
                w.append(M[f"{tag}_utev{t - 2}"])
            elif tb_gate is not None:
                w.append(tb_gate)

            def tr_emit(h, t=t, p=p):
                for k in range(KC):
                    i = h.transpose(tb[p][:, k * 128:(k + 1) * 128],
                                    u_tile[:, p * D + k * 128:p * D + (k + 1) * 128],
                                    ident[:, :])
                return i
            SPE.v += 1
            b.op("pe", tr_emit, w, ((SPE, 1),))
            M[f"{tag}_utr{t}"] = SPE.now()
            out_ap = uT.rearrange("p (k s) -> p k s", s=TOK)[:, :, t * 128:(t + 1) * 128]
            in_ap = tb[p].rearrange("p (k s) -> p k s", s=128)
            M[f"{tag}_utev{t}"] = dv(
                lambda h, o=out_ap, i=in_ap: h.tensor_copy(o, i),
                [M[f"{tag}_utr{t}"]])

    def emit_router(tag):
        nmx, esum, rcp = sstat[:, 0:1], sstat[:, 1:2], sstat[:, 2:3]
        m1, m2 = sstat[:, 3:4], sstat[:, 4:5]
        for t in range(NT):
            pr = pb[t % 2]

            def mm_emit(h, t=t, pr=pr):
                for k in range(KC):
                    i = h.matmul(pr[:, 0:NE],
                                 uT[:, k * TOK + t * 128:k * TOK + (t + 1) * 128],
                                 wrb[:, k * NE:(k + 1) * NE],
                                 start=(k == 0), stop=(k == KC - 1))
                return i
            w = [M[f"{tag}_utev{t}"], "wr"]
            if t >= 2:
                w.append(M[f"{tag}_rexp{t - 2}"])
            SPE.v += 1
            b.op("pe", mm_emit, w, ((SPE, 1),))
            mm_done = SPE.now()

            def dv(emit, waits=()):
                SDV.v += 1
                b.op("dve", emit, waits, ((SDV, 1),))
                return SDV.now()
            h1 = dv(lambda h, pr=pr: h.reduce_max(nmx, pr[:, 0:NE], AX.X, negate=True),
                    [mm_done])
            SAC.v += 1
            b.op("act", lambda h, pr=pr: h.activation(
                probs[:, :], pr[:, 0:NE], AF.Exp, bias=nmx, scale=1.0,
                accum_out=esum), [h1], ((SAC, 1),))
            M[f"{tag}_rexp{t}"] = SAC.now()
            h2 = dv(lambda h: h.reciprocal(rcp, esum), [SAC.now()])
            h3 = dv(lambda h: h.tensor_scalar_mul(probs[:, :], probs[:, :], rcp), [h2])
            h4 = dv(lambda h: h.reduce_max(m1, probs[:, :], AX.X), [h3])
            h5 = dv(lambda h: h.tensor_scalar(rtmp[:, :], probs[:, :], m1, None,
                                              ALU.is_ge), [h4])
            h6 = dv(lambda h: h.scalar_tensor_tensor(rtmp[:, :], rtmp[:, :], -2.0,
                                                     probs[:, :], ALU.mult, ALU.add), [h5])
            h7 = dv(lambda h: h.reduce_max(m2, rtmp[:, :], AX.X), [h6])
            h8 = dv(lambda h: h.tensor_scalar(rtmp[:, :], probs[:, :], m2, None,
                                              ALU.is_ge), [h7])
            dv(lambda h, t=t: h.scalar_tensor_tensor(
                cw[:, t * NE:(t + 1) * NE], probs[:, :], 0.5, rtmp[:, :],
                ALU.mult, ALU.mult), [h8])
            M[f"{tag}_cw{t}"] = SDV.now()
        M[f"{tag}_router_done"] = SAC.now()
        M[f"{tag}_cw_done"] = SDV.now()

    def emit_units(tag, layer, store=False, out_gate=None, on_final_evac=None,
                   on_group_end=None):
        its = []
        for g in range(NG):
            for t in range(NT):
                for j in range(2):
                    its.append((g, t, j, 2 * g + j))
        NI = len(its)

        def hgmm(i):
            g, t, j, u = its[i]
            s = u % 4
            pg_, pu_ = pb[2 + 2 * (i % 2)], pb[3 + 2 * (i % 2)]
            w = [f"w{layer}_{u}", M[f"{tag}_utev{t}"]]
            if u >= 2:
                w.append(f"{tag}_cw{t}")
            if i >= 2:
                w.append(M[f"{tag}_mul{i - 2}"])

            def emit(h, t=t, s=s, pg_=pg_, pu_=pu_):
                for k in range(KC):
                    h.matmul(pg_[:, :],
                             uT[:, k * TOK + t * 128:k * TOK + (t + 1) * 128],
                             slot_g[s][:, k * 512:(k + 1) * 512],
                             start=(k == 0), stop=(k == KC - 1))
                    i2 = h.matmul(pu_[:, :],
                                  uT[:, k * TOK + t * 128:k * TOK + (t + 1) * 128],
                                  slot_u[s][:, k * 512:(k + 1) * 512],
                                  start=(k == 0), stop=(k == KC - 1))
                return i2
            SPE.v += 1
            b.op("pe", emit, w, ((SPE, 1),))
            M[f"{tag}_hg{i}"] = SPE.now()

        def epi(i):
            g, t, j, u = its[i]
            p = i % 2
            pg_, pu_ = pb[2 + 2 * p], pb[3 + 2 * p]
            w = [M[f"{tag}_hg{i}"]]
            if i >= 2:
                w.append(M[f"{tag}_mul{i - 2}"])
            SAC.v += 1
            b.op("act", lambda h, p=p, pg_=pg_: h.activation(
                sil[:, p * 512:(p + 1) * 512], pg_[:, :], AF.Silu,
                bias=0.0, scale=1.0), w, ((SAC, 1),))
            silu_done = SAC.now()
            scl = 0.5 if u < 2 else cw[:, t * NE + (u - 2):t * NE + (u - 2) + 1]
            w = [silu_done]
            if i >= 2:
                w.append(M[f"{tag}_tr{i - 2}"])
            SDV.v += 1
            b.op("dve", lambda h, p=p, pu_=pu_, scl=scl: h.scalar_tensor_tensor(
                hbuf[:, p * 512:(p + 1) * 512], sil[:, p * 512:(p + 1) * 512],
                scl, pu_[:, :], ALU.mult, ALU.mult), w, ((SDV, 1),))
            M[f"{tag}_mul{i}"] = SDV.now()

        def tr(i):
            p = i % 2
            q3 = i % 3
            w = [M[f"{tag}_mul{i}"]]
            if i >= 2:
                w.append(M[f"{tag}_htev{i - 2}"])
            else:
                w.append(M[f"{tag}_utev{NT - 1}"])

            def emit(h, p=p):
                for k in range(4):
                    i2 = h.transpose(tb[p][:, k * 128:(k + 1) * 128],
                                     hbuf[:, p * 512 + k * 128:p * 512 + (k + 1) * 128],
                                     ident[:, :])
                return i2
            SPE.v += 1
            b.op("pe", emit, w, ((SPE, 1),))
            M[f"{tag}_tr{i}"] = SPE.now()
            w = [M[f"{tag}_tr{i}"]]
            if i >= 3:
                w.append(M[f"{tag}_wd{i - 3}"])
            SDV.v += 1
            b.op("dve", lambda h, q3=q3, p=p: h.tensor_copy(
                hT[:, q3 * 512:(q3 + 1) * 512], tb[p][:, 0:512]), w, ((SDV, 1),))
            M[f"{tag}_htev{i}"] = SDV.now()

        def wd(i):
            g, t, j, u = its[i]
            s = u % 4
            p = i % 3
            w = [M[f"{tag}_htev{i}"]]
            if i < 2:
                w.append(f"{tag}_router_done")
                w.append(f"{tag}_cw_done")
            if j == 0:
                if t > 0:
                    w.append(M[f"{tag}_oev{g}_{t - 1}"])
                elif g > 0:
                    w.append(M[f"{tag}_oev{g - 1}_{NT - 1}"])
                elif out_gate is not None:
                    w.append(out_gate)

            def emit(h, s=s, p=p, j=j):
                for k in range(4):
                    st = (j == 0 and k == 0)
                    sp = (j == 1 and k == 3)
                    h.matmul(pb[0][:, :],
                             hT[:, p * 512 + k * 128:p * 512 + (k + 1) * 128],
                             slot_d[s][:, k * D:k * D + 512], start=st, stop=sp)
                    i2 = h.matmul(pb[1][:, :],
                                  hT[:, p * 512 + k * 128:p * 512 + (k + 1) * 128],
                                  slot_d[s][:, k * D + 512:(k + 1) * D],
                                  start=st, stop=sp)
                return i2
            SPE.v += 1
            b.op("pe", emit, w, ((SPE, 1),))
            M[f"{tag}_wd{i}"] = SPE.now()
            if j == 1:
                M[f"{tag}_wdstop{g}_{t}"] = SPE.now()
                if t == NT - 1 and on_group_end is not None:
                    on_group_end(g)

                def ev(h, t=t):
                    h.tensor_add(xsb[:, t * D:t * D + 512],
                                 xsb[:, t * D:t * D + 512], pb[0][:, :])
                    return h.tensor_add(xsb[:, t * D + 512:(t + 1) * D],
                                        xsb[:, t * D + 512:(t + 1) * D], pb[1][:, :])
                SDV.v += 1
                b.op("dve", ev, [M[f"{tag}_wdstop{g}_{t}"]], ((SDV, 1),))
                M[f"{tag}_oev{g}_{t}"] = SDV.now()
                if on_final_evac is not None and g == NG - 1:
                    on_final_evac(t, [M[f"{tag}_oev{g}_{t}"]])
                if store and g == NG - 1:
                    dma("sp", xout.ap()[t * 128:(t + 1) * 128, :],
                        xsb[:, t * D:(t + 1) * D], [M[f"{tag}_oev{g}_{t}"]],
                        sem=SXO)

        for i in range(NI):
            hgmm(i)
            epi(i)
            if i >= 1:
                tr(i - 1)
            if i >= 2:
                wd(i - 2)
        tr(NI - 1)
        wd(NI - 2)
        wd(NI - 1)
        M[f"{tag}_done"] = SDV.now()

    def make_early_stats(src_fn, dump_ap):
        sums, sqa, sqb = lnstat[:, 0:16], lnstat[:, 16:32], lnstat[:, 32:48]
        state = {"pair": None}

        def emit(t, gate_w):
            src = src_fn(t)
            w = list(gate_w)
            wsq = list(w)
            if state["pair"] is not None:
                wsq.append(state["pair"])
            SAC.v += 1
            b.op("act", lambda h, src=src: h.activation(
                dump_ap, src[:, 0:512], AF.Square,
                bias=0.0, scale=1.0, accum_out=sqa[:, t:t + 1]), wsq, ((SAC, 1),))
            first = SAC.now()
            SAC.v += 1
            b.op("act", lambda h, src=src: h.activation(
                dump_ap, src[:, 512:D], AF.Square,
                bias=0.0, scale=1.0, accum_out=sqb[:, t:t + 1]), [first], ((SAC, 1),))
            state["pair"] = SAC.now()
            SDV.v += 1
            b.op("dve", lambda h, src=src, t=t: h.reduce_sum(
                sums[:, t:t + 1], src[:, :], AX.X), w, ((SDV, 1),))
            state["red"] = SDV.now()

        def done():
            return (state["pair"], state["red"])
        return emit, done

    emit_ln("L1", [xin_done] * NT)
    emit_router("L1")
    if PARTS == "ffn1":
        def l1_groups_f(g):
            if g + 2 >= NG:
                return
            for j in range(2):
                u = 2 * (g + 2) + j
                load_unit(u, u % 4, [f"L1_wdstop{g}_{NT - 1}"], 1)
        emit_units("L1", 1, store=True, on_group_end=l1_groups_f)
    elif True:
        l2st_emit, l2st_done = make_early_stats(
            lambda t: xsb[:, t * D:(t + 1) * D], u_tile[:, 0:512])
        spill_state = {"prev": None}

        def l1_post(t, gate_w):
            l2st_emit(t, gate_w)
            wsp = list(gate_w)
            if spill_state["prev"] is not None:
                wsp.append(spill_state["prev"])
            spill_state["prev"] = dma(
                "sp", xspill.ap()[t * 128:(t + 1) * 128, :],
                xsb[:, t * D:(t + 1) * D], wsp, sem=SXSP)
            M[f"spl{t}"] = spill_state["prev"]
        def l1_groups(g):
            if g + 2 >= NG:
                return
            for j in range(2):
                u = 2 * (g + 2) + j
                wld = [f"L1_wdstop{g}_{NT - 1}"]
                if PARTS == "full" and u >= 4:
                    wld.append(f"wb{u - 4}")
                load_unit(u, u % 4, wld, 1)
        emit_units("L1", 1, store=False, on_final_evac=l1_post,
                   on_group_end=l1_groups)

    if PARTS == "ffn1":
        b.op("sp", None, [(SXO, SXO.v)])
        finish(nc, b, sems)
        return nc

    emit_ln("L2", [None] * NT,
            first_gate="L1_done", tb_gate="L1_done", stats=l2st_done())

    blk_order = list(range(8, 16)) + list(range(16, 24)) + list(range(8))
    for pos, blk in enumerate(blk_order):
        p = pos % 2
        w = ["L1_wdstop7_15"]
        if PARTS == "full":
            w.append("wb15")
        if pos >= 2:
            w.append(M[f"wld{pos - 2}"])
        M[f"wld{pos}"] = dma("gp", wstbuf[:, pos * D:(pos + 1) * D],
                             wqkv.ap()[blk * 128:(blk + 1) * 128, :], w, sem=SST[p])
    for pos, blk in enumerate(blk_order):
        p = pos % 2
        w = [M[f"wld{pos}"], "ident",
             M["L2_utev15"] if pos < 2 else M[f"wstev{blk_order[pos - 2]}"]]

        def tre(h, p=p, pos=pos):
            for k in range(KC):
                i2 = h.transpose(tb[p][:, k * 128:(k + 1) * 128],
                                 wstbuf[:, pos * D + k * 128:pos * D + (k + 1) * 128],
                                 ident[:, :])
            return i2
        SPE.v += 1
        b.op("pe", tre, w, ((SPE, 1),))
        M[f"wst{blk}"] = SPE.now()
        out_ap = wqkvT.rearrange("p (k e) -> p k e", e=3 * D)[:, :, blk * 128:(blk + 1) * 128]
        in_ap = tb[p].rearrange("p (k s) -> p k s", s=128)
        w2 = [M[f"wst{blk}"]]
        if pos == 0:
            w2.append(M["spl11"])
        SDV.v += 1
        b.op("dve", lambda h, o=out_ap, i=in_ap: h.tensor_copy(o, i),
             w2, ((SDV, 1),))
        M[f"wstev{blk}"] = SDV.now()
    wst_all = SDV.now()

    if PARTS == "attn_ln":
        b.op("sp", None, [wst_all, spill_done])
        finish(nc, b, sems)
        return nc

    idx = 0
    for dk in range(KC):
        for tbk in range(4):
            pp = pb[2 + (idx % 2)]
            w = [M[f"wstev{8 + dk}"], M[f"L2_utev{4 * tbk + 3}"]]
            if idx >= 2:
                w.append(M[f"kev{idx - 2}"])

            def kmm(h, dk=dk, tbk=tbk, pp=pp):
                for k in range(KC):
                    i2 = h.matmul(
                        pp[:, :],
                        wqkvT[:, k * 3 * D + D + dk * 128:k * 3 * D + D + (dk + 1) * 128],
                        uT[:, k * TOK + tbk * 512:k * TOK + (tbk + 1) * 512],
                        start=(k == 0), stop=(k == KC - 1))
                return i2
            SPE.v += 1
            b.op("pe", kmm, w, ((SPE, 1),))
            mmd = SPE.now()
            wkev = [mmd]
            if idx < 2 and PARTS == "full":
                wkev.append("wb17")
            SDV.v += 1
            b.op("dve", lambda h, dk=dk, tbk=tbk, pp=pp: h.tensor_copy(
                ktown[:, dk * TOK + tbk * 512:dk * TOK + (tbk + 1) * 512], pp[:, :]),
                wkev, ((SDV, 1),))
            M[f"kev{idx}"] = SDV.now()
            idx += 1
    agk_done = []
    k_staged = []
    for hh in range(4):
        wks = [M[f"kev{7 + 8 * hh}"]] + ([k_staged[-1]] if hh > 0 else [])
        ks = dma("sp", agk_in[hh].ap(), ktown[:, hh * 4096:(hh + 1) * 4096],
                 wks, sem=SAGK)
        k_staged.append(ks)
        if FAKE_CC:
            dma("gp", agk_out[hh].ap()[0:128, :], agk_in[hh].ap(), [ks], sem=SW)
            agk_done.append(dma("gp", agk_out[hh].ap()[128:256, :], agk_in[hh].ap(),
                                [ks], sem=SW))
        else:
            SCCS[hh].v += 1
            b.op("gp", lambda h, hh=hh: h.collective_compute(
                "AllGather", ALU.bypass, replica_groups=RG,
                ins=[agk_in[hh].ap().opt()], outs=[agk_out[hh].ap().opt()]),
                [ks], ((SCCS[hh], 1),))
            agk_done.append(SCCS[hh].now())

    for t in range(NT):
        pA, pB = (pb[2], pb[3]) if t % 2 == 0 else (pb[4], pb[5])
        w = [M["wstev23"]]
        w += [M["kev31"]] if t < 2 else [M[f"vev{t - 2}"]]

        def vmm(h, t=t, pA=pA, pB=pB):
            for k in range(KC):
                h.matmul(pA[:, :], uT[:, k * TOK + t * 128:k * TOK + (t + 1) * 128],
                         wqkvT[:, k * 3 * D + 2 * D:k * 3 * D + 2 * D + 512],
                         start=(k == 0), stop=(k == KC - 1))
                i2 = h.matmul(pB[:, :], uT[:, k * TOK + t * 128:k * TOK + (t + 1) * 128],
                              wqkvT[:, k * 3 * D + 2 * D + 512:k * 3 * D + 3 * D],
                              start=(k == 0), stop=(k == KC - 1))
            return i2
        SPE.v += 1
        b.op("pe", vmm, w, ((SPE, 1),))
        mmd = SPE.now()

        def vev(h, t=t, pA=pA, pB=pB):
            h.tensor_copy(vown[:, t * D:t * D + 512], pA[:, :])
            return h.tensor_copy(vown[:, t * D + 512:(t + 1) * D], pB[:, :])
        wvev = [mmd]
        if t == 0 and PARTS == "full":
            wvev.append("wb17")
        SDV.v += 1
        b.op("dve", vev, wvev, ((SDV, 1),))
        M[f"vev{t}"] = SDV.now()
    agv_done = []
    v_staged = []
    for hh in range(2):
        wvs = [M[f"vev{7 + 8 * hh}"]] + ([v_staged[0]] if hh == 1 else [])
        vs = dma("sp", agv_in[hh].ap(), vown[:, hh * 8192:(hh + 1) * 8192],
                 wvs, sem=SAGV)
        v_staged.append(vs)
        if FAKE_CC:
            dma("gp", agv_out[hh].ap()[0:128, :], agv_in[hh].ap(), [vs], sem=SW)
            agv_done.append(dma("gp", agv_out[hh].ap()[128:256, :], agv_in[hh].ap(),
                                [vs], sem=SW))
        else:
            SCCS[4 + hh].v += 1
            b.op("gp", lambda h, hh=hh: h.collective_compute(
                "AllGather", ALU.bypass, replica_groups=RG,
                ins=[agv_in[hh].ap().opt()], outs=[agv_out[hh].ap().opt()]),
                [vs], ((SCCS[4 + hh], 1),))
            agv_done.append(SCCS[4 + hh].now())

    idx = 0
    for tbk in range(4):
        for dq in range(KC):
            pp = pb[2 + (idx % 2)]
            w = [M[f"wstev{dq}"]]
            w += [M[f"vev{14 + idx}"]] if idx < 2 else [M[f"qev{idx - 2}"]]

            def qmm(h, dq=dq, tbk=tbk, pp=pp):
                for k in range(KC):
                    i2 = h.matmul(
                        pp[:, :],
                        wqkvT[:, k * 3 * D + dq * 128:k * 3 * D + (dq + 1) * 128],
                        uT[:, k * TOK + tbk * 512:k * TOK + (tbk + 1) * 512],
                        start=(k == 0), stop=(k == KC - 1))
                return i2
            SPE.v += 1
            b.op("pe", qmm, w, ((SPE, 1),))
            mmd = SPE.now()
            SDV.v += 1
            b.op("dve", lambda h, dq=dq, tbk=tbk, pp=pp: h.tensor_scalar_mul(
                qT[:, dq * TOK + tbk * 512:dq * TOK + (tbk + 1) * 512],
                pp[:, :], 1.0 / 32.0), [mmd], ((SDV, 1),))
            M[f"qev{tbk * 8 + dq}"] = SDV.now()
            idx += 1
    q_done = SDV.now()
    M["qproj_pe"] = SPE.now()

    ktf_v = ktf.rearrange("p (k s) -> p k s", s=SEQ)
    for hh in range(4):
        ko_v0 = agk_out[hh].ap()[0:128, :].rearrange("p (k s) -> p k s", s=TOK)
        ko_v1 = agk_out[hh].ap()[128:256, :].rearrange("p (k s) -> p k s", s=TOK)
        wv = [agk_done[hh], k_staged[3]] if hh < 2 else [agk_done[hh], v_staged[1]]
        r0 = dma("sp", ktf_v[:, 2 * hh:2 * hh + 2, 0:TOK], ko_v0, wv, sem=SKV[hh])
        r = dma("sp", ktf_v[:, 2 * hh:2 * hh + 2, TOK:SEQ], ko_v1, wv + [r0],
                sem=SKV[hh])
        M[f"ktfQ{hh}"] = r
    for hh in range(2):
        wv = [agv_done[hh], "qproj_pe", "spl15"]
        r0 = dma("sp", va[:, hh * 8192:(hh + 1) * 8192], agv_out[hh].ap()[0:128, :],
                 wv, sem=SKV[4 + hh])
        r = dma("sp", vb[:, hh * 8192:(hh + 1) * 8192], agv_out[hh].ap()[128:256, :],
                wv + [r0], sem=SKV[4 + hh])
        if hh == 0:
            M["vA"] = r
        else:
            M["vB"] = r

    if PARTS == "attn_qkv":
        b.op("sp", None, [kv_loaded, q_done])
        finish(nc, b, sems)
        return nc

    if PARTS == "full":
        l3st_emit, l3st_done = make_early_stats(lambda t: x2st[:, :], sqdump[:, :])

    pmax, psums = sstat[:, 8:16], sstat[:, 16:24]
    nmx_a, ssum, rinv = sstat[:, 24:25], sstat[:, 25:26], sstat[:, 26:27]

    def attn_scores(q, half):
        if True:
            for dc in range(KC):
                w = []
                if q == 0 and half == 0 and dc == 0:
                    w = [M["ktfQ0"], q_done]
                if q == 0 and half == 0 and dc in (2, 4, 6):
                    w = [M[f"ktfQ{dc // 2}"]]
                if dc == 0 and (q > 0 or half > 0):
                    pq_, ph_ = (q, 0) if half == 1 else (q - 1, 1)
                    w.append(M[f"sev{pq_}_{ph_}"])

                def smm(h, q=q, half=half, dc=dc):
                    for ktl in range(4):
                        i2 = h.matmul(
                            pb[2 + ktl][:, :],
                            qT[:, dc * TOK + q * 128:dc * TOK + (q + 1) * 128],
                            ktf[:, dc * SEQ + (half * 4 + ktl) * 512:
                                dc * SEQ + (half * 4 + ktl + 1) * 512],
                            start=(dc == 0), stop=(dc == KC - 1))
                    return i2
                if dc == KC - 1:
                    SPE.v += 1
                    b.op("pe", smm, w, ((SPE, 1),))
                else:
                    b.op("pe", smm, w)
            stop_m = SPE.now()
            w = [stop_m]
            if q > 0:
                w.append(f"nmx{q - 1}")
                w.append(M[f"exp{q - 1}_7"])

            for ktl in range(4):
                kt = half * 4 + ktl
                SAC.v += 1
                b.op("act", lambda h, kt=kt, ktl=ktl: h.activation(
                    scores[:, kt * 512:(kt + 1) * 512],
                    pb[2 + ktl][:, :], AF.Copy, bias=0.0, scale=1.0),
                    w if ktl == 0 else (), ((SAC, 1),))
                M[f"sev{q}_{half}_{ktl}"] = SAC.now()
            M[f"sev{q}_{half}"] = SAC.now()

    def attn_pmax(q, half):
        for ktl in range(4):
            kt = 4 * half + ktl
            SDV.v += 1
            b.op("dve", lambda h, kt=kt: h.reduce_max(
                pmax[:, kt:kt + 1], scores[:, kt * 512:(kt + 1) * 512], AX.X),
                [M[f"sev{q}_{half}_{ktl}"]], ((SDV, 1),))
        M[f"pmx{q}_{half}"] = SDV.now()

    def attn_softmax_part(q):
        def dv(emit, waits=()):
            SDV.v += 1
            b.op("dve", emit, waits, ((SDV, 1),))
            return SDV.now()

        M[f"nmx{q}"] = dv(lambda h: h.reduce_max(nmx_a, pmax[:, :], AX.X, negate=True),
                          [M[f"pmx{q}_0"], M[f"pmx{q}_1"]])
        for kt in range(8):
            w = [M[f"nmx{q}"]]
            if kt == 0 and q > 0:
                w.append(M[f"ptr{q - 1}"])
                w.append(M[f"rinv{q - 1}"])
            SAC.v += 1
            b.op("act", lambda h, kt=kt: h.activation(
                pbuf[:, kt * 512:(kt + 1) * 512], scores[:, kt * 512:(kt + 1) * 512],
                AF.Exp, bias=nmx_a, scale=1.0), w, ((SAC, 1),))
            M[f"exp{q}_{kt}"] = SAC.now()



    def attn_ptpv_part(q):
        def dv(emit, waits=()):
            SDV.v += 1
            b.op("dve", emit, waits, ((SDV, 1),))
            return SDV.now()

        banks = [tb[0], tb[1], pbf[0], pbf[1]]
        for bk in range(4):
            bank = banks[bk]
            for half2 in range(2):
                jj = 2 * bk + half2
                w = [M[f"exp{q}_{jj}"], "ident"]
                if half2 == 0:
                    if q > 0:
                        w.append(M[f"pteb{q - 1}_{bk}"])
                    else:
                        w.append(M["qev7"])
                    if bk >= 2 and q > 0:
                        w.append(M[f"oev{q - 1}"])

                def ptr(h, jj=jj, bank=bank, half2=half2):
                    for m in range(4):
                        i2 = h.transpose(
                            bank[:, half2 * 512 + m * 128:half2 * 512 + (m + 1) * 128],
                            pbuf[:, (4 * jj + m) * 128:(4 * jj + m + 1) * 128],
                            ident[:, :])
                    return i2
                SPE.v += 1
                b.op("pe", ptr, w, ((SPE, 1),))
                trd = SPE.now()
            if bk == 3:
                M[f"ptr{q}"] = trd
            w = [trd]
            if q > 0:
                w.append(M[f"pv{q - 1}"])
            dv(lambda h, bk=bk, bank=bank: h.tensor_copy(
                ptb[:, bk * 1024:(bk + 1) * 1024], bank[:, :]), w)
            M[f"pteb{q}_{bk}"] = SDV.now()
        pte_all = SDV.now()
        pte_all_a = None
        if PARTS == "s_pt":
            return

        w = [pte_all]
        if q > 0:
            w.append(M[f"oev{q - 1}"])

        order = list(range(0, 8)) + list(range(16, 24)) + \
            list(range(8, 16)) + list(range(24, 32))

        def pv1(h, q=q):
            for n2, kc2 in enumerate(order[:16]):
                vsrc = va if kc2 < 16 else vb
                off = (kc2 % 16) * D
                h.matmul(pb[0][:, :], ptb[:, kc2 * 128:(kc2 + 1) * 128],
                         vsrc[:, off:off + 512], start=(n2 == 0), stop=False)
                i2 = h.matmul(pb[1][:, :], ptb[:, kc2 * 128:(kc2 + 1) * 128],
                              vsrc[:, off + 512:off + D], start=(n2 == 0), stop=False)
            return i2

        def pv2(h, q=q):
            for n2, kc2 in enumerate(order[16:]):
                vsrc = va if kc2 < 16 else vb
                off = (kc2 % 16) * D
                h.matmul(pb[0][:, :], ptb[:, kc2 * 128:(kc2 + 1) * 128],
                         vsrc[:, off:off + 512], start=False, stop=(n2 == 15))
                i2 = h.matmul(pb[1][:, :], ptb[:, kc2 * 128:(kc2 + 1) * 128],
                              vsrc[:, off + 512:off + D], start=False, stop=(n2 == 15))
            return i2
        if q == 0:
            w.append(M["vA"])
        b.op("pe", pv1, w)
        w2 = [M["vB"]] if q == 0 else []
        SPE.v += 1
        b.op("pe", pv2, w2, ((SPE, 1),))
        M[f"pv{q}"] = SPE.now()

    def attn_oev(q):
        p = q % 2
        SDV.v += 1
        b.op("dve", lambda h: h.reduce_sum(ssum, pbuf[:, :], AX.X),
             [M[f"exp{q}_7"]], ((SDV, 1),))
        h2 = SDV.now()
        SDV.v += 1
        b.op("dve", lambda h: h.reciprocal(rinv, ssum), [h2], ((SDV, 1),))
        M[f"rinv{q}"] = SDV.now()
        w = [M[f"pv{q}"], M[f"x1l{q}"], M[f"rinv{q}"]]
        if q >= 1:
            w.append(M[f"x2sp{q - 1}"])
            if PARTS == "full":
                w.append(M[f"l3sq{q - 1}"])
            if PARTS == "attn":
                w.append(M[f"xo{q - 1}"])

        def ev(h, p=p):
            h.scalar_tensor_tensor(x2st[:, 0:512], pb[0][:, :], rinv,
                                   x1s[:, p * D:p * D + 512], ALU.mult, ALU.add)
            return h.scalar_tensor_tensor(x2st[:, 512:1024], pb[1][:, :], rinv,
                                          x1s[:, p * D + 512:(p + 1) * D],
                                          ALU.mult, ALU.add)
        SDV.v += 1
        b.op("dve", ev, w, ((SDV, 1),))
        M[f"oev{q}"] = SDV.now()
        if PARTS == "full":
            l3st_emit(q, [M[f"oev{q}"]])
            M[f"l3sq{q}"] = (SAC, SAC.v)
        if PARTS == "attn":
            M[f"xo{q}"] = dma("sp", xout.ap()[q * 128:(q + 1) * 128, :], x2st[:, :],
                              [M[f"oev{q}"]], sem=SXO)
        M[f"x2sp{q}"] = dma("sp", x2spill.ap()[q * 128:(q + 1) * 128, :],
                            x2st[:, :], [M[f"oev{q}"]], sem=SX2)

    if PARTS in ("s_sc", "s_sm", "s_pt", "s_pv"):
        attn_scores(0, 0)
        attn_pmax(0, 0)
        attn_scores(0, 1)
        attn_pmax(0, 1)
        M["x1l0"] = dma("sp", x1s[:, 0:D], xspill.ap()[0:128, :],
                        [M["spl0"], "L2_utev15"], sem=SX1[0])
        if PARTS == "s_sc":
            b.op("sp", None, [M["sev0_1"]])
            finish(nc, b, sems); return nc
        attn_softmax_part(0)
        if PARTS == "s_sm":
            b.op("sp", None, [M["rinv0"]])
            finish(nc, b, sems); return nc
        attn_ptpv_part(0)
        if PARTS == "s_pt":
            b.op("sp", None, [M["pte0_7"] if "pte0_7" in M else M["pv0"]])
            finish(nc, b, sems); return nc
        attn_oev(0)
        b.op("sp", None, [(SX2, SX2.v)])
        finish(nc, b, sems); return nc

    NQ = NT if NQLIM is None else NQLIM
    for q in range(NQ):
        if q > 0:
            attn_softmax_part(q - 1)
        attn_scores(q, 0)
        w = [M[f"spl{q}"], "L2_utev15"] if q < 2 else [f"oev{q - 2}"]
        M[f"x1l{q}"] = dma("sp", x1s[:, (q % 2) * D:(q % 2 + 1) * D],
                           xspill.ap()[q * 128:(q + 1) * 128, :], w, sem=SX1[q % 2])
        if q > 0:
            attn_ptpv_part(q - 1)
        attn_pmax(q, 0)
        attn_scores(q, 1)
        attn_pmax(q, 1)
        if q > 0:
            attn_oev(q - 1)
    attn_softmax_part(NQ - 1)
    attn_ptpv_part(NQ - 1)
    attn_oev(NQ - 1)
    M["L2_dve_end"] = SDV.now()
    if NQLIM is not None:
        b.op("sp", None, [(SX2, SX2.v)])
        finish(nc, b, sems)
        return nc

    if PARTS == "attn":
        b.op("sp", None, [(SXO, SXO.v), (SX2, SX2.v)])
        finish(nc, b, sems)
        return nc

    pv_last = M[f"pv{NT - 1}"]
    xin3_parts = []
    for c4 in range(8):
        wpre = [pv_last, M[f"x2sp{2 * c4 + 1}"]]
        if c4 >= 2:
            wpre.append(xin3_parts[c4 - 2])
        xin3_parts.append(dma(
            "sp",
            xsb.rearrange("p (t d) -> p t d", d=D)[:, c4 * 2:(c4 + 1) * 2, :],
            x2spill.rearrange("(t p) d -> p t d", p=128)[:, c4 * 2:(c4 + 1) * 2, :],
            wpre, sem=SPF[c4 % 2]))
    xin3_done = xin3_parts[-1]
    for u in range(4):
        load_unit(u, u, [M["sev15_1"]], 3)

    emit_ln("L3", [xin3_parts[t // 2] for t in range(NT)],
            first_gate="L2_dve_end", tb_gate="L2_dve_end", stats=l3st_done())
    emit_router("L3")
    emit_units("L3", 3, store=True, out_gate=f"oev{NT - 1}")
    for g in range(2, NG):
        for j in range(2):
            u = 2 * g + j
            load_unit(u, u % 4, [f"L3_wdstop{g - 2}_{NT - 1}"], 3)

    b.op("sp", None, [(SXO, SXO.v)])
    finish(nc, b, sems)
    return nc


def finish(nc, b, sems):
    from contextlib import ExitStack
    ctx = ExitStack()
    for s in sems:
        s.h = ctx.enter_context(nc.semaphore(s.name))
    eng_map = {"sp": "sync", "gp": "gpsimd", "pe": "tensor", "dve": "vector",
               "act": "scalar"}
    with nc.Block() as block:
        for ename, attr in eng_map.items():
            def run(h, ename=ename):
                last = {}
                for (e, waits, emit, incs) in b.ops:
                    if e != ename:
                        continue
                    for wref in waits:
                        w = b.resolve(wref)
                        if w is None:
                            continue
                        s, v = w
                        if v > 0 and last.get(s, -1) < v:
                            h.wait_ge(s.h, v)
                            last[s] = v
                    if emit is None:
                        continue
                    inst = emit(h)
                    for (s, amt) in incs:
                        inst.then_inc(s.h, amt)
            getattr(block, attr)(run)
    ctx.close()


def kernel(L0_attn_x, L0_attn_W_qkv, L0_ffn_Wr, L0_ffn_Wg, L0_ffn_Wu, L0_ffn_Wd,
           L0_ffn_Sg, L0_ffn_Su, L0_ffn_Sd):
    x = np.ascontiguousarray(np.asarray(L0_attn_x, dtype=np.float32)).reshape(8, TOK, D)
    com = {
        "wqkv": np.ascontiguousarray(np.asarray(L0_attn_W_qkv, dtype=np.float32)),
        "wr": np.ascontiguousarray(np.asarray(L0_ffn_Wr, dtype=np.float32)),
        "wgh": np.ascontiguousarray(np.asarray(L0_ffn_Wg, dtype=np.float32)),
        "wuh": np.ascontiguousarray(np.asarray(L0_ffn_Wu, dtype=np.float32)),
        "wdh": np.ascontiguousarray(np.asarray(L0_ffn_Wd, dtype=np.float32)),
        "sgh": np.ascontiguousarray(np.asarray(L0_ffn_Sg, dtype=np.float32)),
        "suh": np.ascontiguousarray(np.asarray(L0_ffn_Su, dtype=np.float32)),
        "sdh": np.ascontiguousarray(np.asarray(L0_ffn_Sd, dtype=np.float32)),
    }
    nc = build_program()
    in_maps = [dict(com, xin=np.ascontiguousarray(x[c])) for c in range(8)]
    res = run_bass_kernel_spmd(nc, in_maps, core_ids=list(range(8)))
    out = np.concatenate([res.results[c]["xout"][None] for c in range(8)], axis=0)
    return out.reshape(4, 4096, D)
